# version 24
# baseline (speedup 1.0000x reference)
from contextlib import ExitStack

import numpy as np
import ml_dtypes

import concourse.bass as bass
import concourse.mybir as mybir
from concourse.bass_utils import run_bass_kernel_spmd

F32 = mybir.dt.float32
BF16 = mybir.dt.bfloat16
AF = mybir.ActivationFunctionType
ALU = mybir.AluOpType
AX = mybir.AxisListType

B, T, D = 4, 4096, 1024
OWN = 2048
NT = 16
NKV = 8
NCH = 8
NEXP, FI = 16, 512
NJ = 8
EPS = 1e-5
QK_SCALE = 1.0 / 8192.0
AV_SCALE = 1.0 / 1024.0

ENGINES = ("pe", "act", "dve", "gp", "sp")


class Buf:
    __slots__ = ("name", "writer", "readers")

    def __init__(self, name):
        self.name = name
        self.writer = None
        self.readers = {}


class Sched:

    DPOOL = {"dx": 48, "dg": 12}

    def __init__(self):
        self.ops = []
        self.ecnt = {e: 0 for e in ENGINES}
        self.dcnt = {}
        self.didx = {}

    def op(self, eng, emit, reads=(), writes=(), dma=None, drain=False,
           drain_before=False):
        waits = {}

        def need(tok):
            if tok is None:
                return
            s, v = tok
            if s == eng and eng == "pe":
                return
            if waits.get(s, 0) < v:
                waits[s] = v

        for b in reads:
            need(b.writer)
        for b in writes:
            need(b.writer)
            for t in b.readers.items():
                need(t)
        if dma is None:
            self.ecnt[eng] += 1
            tok = (eng, self.ecnt[eng])
        else:
            i = self.didx.get(dma, 0)
            self.didx[dma] = (i + 1) % self.DPOOL[dma]
            key = f"d:{dma}{i}"
            prev = self.dcnt.get(key, 0)
            if prev:
                if waits.get(key, 0) < prev:
                    waits[key] = prev
            self.dcnt[key] = prev + 16
            tok = (key, self.dcnt[key])
        for b in reads:
            if b.readers.get(tok[0], 0) < tok[1]:
                b.readers[tok[0]] = tok[1]
        for b in writes:
            b.writer = tok
            b.readers = {}
        self.ops.append((eng, emit, waits, tok, dma,
                         (drain_before, drain)))
        return tok

    def barrier(self):
        toks = [(e, self.ecnt[e]) for e in ENGINES if self.ecnt[e] > 0]
        toks += list(self.dcnt.items())
        for e in ENGINES:
            waits = {s: v for s, v in toks if s != e}
            self.ecnt[e] += 1
            self.ops.append((e, None, waits, (e, self.ecnt[e]), None,
                             (False, False)))

    def final_wait(self, eng, queue):
        waits = {k: v for k, v in self.dcnt.items()
                 if k.startswith("d:" + queue) and v}
        if waits:
            self.ecnt[eng] += 1
            self.ops.append((eng, None, waits, (eng, self.ecnt[eng]), None,
                             (False, False)))

    def prepare(self):
        needed = set()
        for o in self.ops:
            needed.update(o[2].items())
        incmap = {}
        for e in ENGINES:
            vals = sorted(v for (s, v) in needed if s == e)
            for newv, origv in enumerate(vals, 1):
                incmap[(e, origv)] = newv
        self._needed = needed
        self._incmap = incmap

    def emit_engine(self, e, h, sems):
        dw = {}
        for _e, emit, waits, tok, dma, drain in self.ops:
            if _e != e:
                continue
            for s in sorted(waits):
                v = waits[s]
                if not s.startswith("d:"):
                    v = self._incmap[(s, v)]
                if dw.get(s, 0) >= v:
                    continue
                dw[s] = v
                h.wait_ge(sems[s], v)
            if emit is None:
                continue
            if drain[0]:
                h.drain()
            inst = emit(h)
            if drain[1]:
                d = h.drain()
                if d is not None:
                    inst = d
            if dma is not None:
                inst.then_inc(sems[tok[0]], 16)
            elif (e, tok[1]) in self._needed:
                inst.then_inc(sems[e], 1)


def _causal_masks(g):
    p = np.arange(128)[:, None, None, None]
    m = np.arange(4)[None, :, None, None]
    i4 = np.arange(4)[None, None, :, None]
    ql = np.arange(128)[None, None, None, :]
    okA = (128 * m + p) <= (256 * i4 + 128 * g + ql)
    i2 = np.arange(2, 4)[None, None, :, None]
    okB = (512 + 128 * m + p) <= (256 * i2 + 128 * g + ql)
    maskA = np.where(okA, 0.0, -1e30).astype(np.float32).reshape(128, 2048)
    maskB = np.where(okB, 0.0, -1e30).astype(np.float32).reshape(128, 1024)
    return maskA, maskB


def build_nc(stage="full"):
    nc = bass.Bass("TRN2", target_bir_lowering=False)

    d_xq = nc.dram_tensor("x_q", [OWN, D], F32, kind="ExternalInput").ap()
    d_xkv = nc.dram_tensor("x_kv", [T, D], F32, kind="ExternalInput").ap()
    d_wq = nc.dram_tensor("w_q", [D, D], F32, kind="ExternalInput").ap()
    d_wk = nc.dram_tensor("w_k", [D, D], F32, kind="ExternalInput").ap()
    d_wv = nc.dram_tensor("w_v", [D, D], F32, kind="ExternalInput").ap()
    d_wp = nc.dram_tensor("w_p", [D, D], F32, kind="ExternalInput").ap()
    d_wm = nc.dram_tensor("w_m", [D, D], F32, kind="ExternalInput").ap()
    d_wo = nc.dram_tensor("w_o", [D, D], F32, kind="ExternalInput").ap()
    d_w1 = nc.dram_tensor("w_1", [NEXP, 128, FI], F32, kind="ExternalInput").ap()
    d_w2r = nc.dram_tensor("w_2", [NEXP, 128, FI], F32,
                           kind="ExternalInput").ap()
    d_b1 = nc.dram_tensor("b_1", [NEXP, 128, 4], F32, kind="ExternalInput").ap()
    d_b2 = nc.dram_tensor("b_2", [NEXP, 128], F32, kind="ExternalInput").ap()
    d_bm = nc.dram_tensor("b_m", [128, 8], F32, kind="ExternalInput").ap()
    d_bo = nc.dram_tensor("b_o", [1, D], F32, kind="ExternalInput").ap()
    d_emb = nc.dram_tensor("emb", [128, NEXP], F32, kind="ExternalInput").ap()
    d_maskA = nc.dram_tensor("maskA", [128, 2048], F32, kind="ExternalInput").ap()
    d_maskB = nc.dram_tensor("maskB", [128, 1024], F32, kind="ExternalInput").ap()
    d_ident = nc.dram_tensor("ident", [128, 128], F32, kind="ExternalInput").ap()
    d_sel8 = nc.dram_tensor("sel8", [8, 512], F32, kind="ExternalInput").ap()
    d_onesel = nc.dram_tensor("onesel", [128, 64], F32, kind="ExternalInput").ap()
    d_sel16 = nc.dram_tensor("sel16", [16, 2048], F32, kind="ExternalInput").ap()
    d_out = nc.dram_tensor("out", [OWN, D], F32, kind="ExternalOutput").ap()
    x2kind = {"kind": "ExternalOutput"} if stage == "oproj" else {}
    d_x2 = nc.dram_tensor("x2scratch", [OWN, D], F32, **x2kind).ap()
    x2_bufs = [Buf(f"x2d{t}") for t in range(NT)]

    ndbg = 4 if stage in ("xnq", "qkv", "attn", "ffn1") else 0
    d_dbg = [nc.dram_tensor(f"dbg{i}", [128, 2048], F32, kind="ExternalOutput").ap()
             for i in range(ndbg)]

    S = Sched()

    pb = [nc.alloc_psum_tensor(f"pb{i}", [128, 512], F32) for i in range(8)]
    pbuf = [Buf(f"pb{i}") for i in range(8)]

    stack0 = ExitStack()

    def mksb(stk):
        def sba(name, shape, dt):
            t = stk.enter_context(nc.sbuf_tensor(name, shape, dt))
            return t, Buf(name)
        return sba

    sb0 = mksb(stack0)

    ident, ident_b = sb0("identity", [128, 128], F32)
    xt = [sb0(f"xt{i}", [128, D], F32) for i in range(2)]
    xn = [sb0(f"xn{i}", [128, D], F32) for i in range(2)]
    stat6 = [sb0(f"stat6_{i}", [128, 12], F32) for i in range(2)]
    mvs = [sb0(f"mv{i}", [128, 2], F32) for i in range(2)]
    negm = [sb0(f"negm{i}", [128, 1], F32) for i in range(2)]
    rstd = [sb0(f"rstd{i}", [128, 1], F32) for i in range(2)]
    x2o = [sb0("x2o0", [128, D], F32)]
    epsc, epsc_b = sb0("epsc", [128, 1], F32)

    def dma(engq, dst_ap, src_ap, reads, writes, q):
        S.op(engq, lambda h, d=dst_ap, s=src_ap: h.dma_start(out=d, in_=s),
             reads=reads, writes=writes, dma=q)

    dma("sp", ident[:, :], d_ident[:, :], (), (ident_b,), "dx")
    S.op("dve", lambda h: h.memset(epsc[:, 0:1], EPS), reads=(), writes=(epsc_b,))

    def ln_tile(i, xt_ap, xt_b):
        s6, s6b = stat6[i]
        mvt, mvb = mvs[i]
        nm, nmb = negm[i]
        rs, rsb = rstd[i]
        xnt, xnb = xn[i % 2]
        S.op("dve", lambda h: h.bn_stats(s6[:, 0:6], xt_ap[:, 0:512]),
             reads=(xt_b,), writes=(s6b,), drain_before=True)
        S.op("dve", lambda h: h.bn_stats(s6[:, 6:12], xt_ap[:, 512:1024]),
             reads=(xt_b,), writes=(s6b,), drain=True)
        S.op("dve", lambda h: h.bn_aggr(mvt[:, 0:2], s6[:, 0:12]),
             reads=(s6b,), writes=(mvb,), drain_before=True, drain=True)
        S.op("act", lambda h: h.activation(out=nm[:, 0:1], in_=mvt[:, 0:1],
                                           func=AF.Copy, scale=-1.0),
             reads=(mvb,), writes=(nmb,))
        S.op("act", lambda h: h.activation(out=rs[:, 0:1], in_=mvt[:, 1:2],
                                           func=AF.Ln, bias=epsc[:, 0:1]),
             reads=(mvb, epsc_b), writes=(rsb,))
        S.op("act", lambda h: h.activation(out=rs[:, 0:1], in_=rs[:, 0:1],
                                           func=AF.Exp, scale=-0.5),
             reads=(rsb,), writes=(rsb,))
        S.op("dve", lambda h: h.tensor_scalar(xnt[:, :], xt_ap[:, :], nm[:, 0:1],
                                              rs[:, 0:1], op0=ALU.add, op1=ALU.mult),
             reads=(xt_b, nmb, rsb), writes=(xnb,))
        return xnt, xnb

    def transpose_bf(src_ap, src_b, bank, dsth, dsthb, dstl, dstlb, dcol):
        pt, ptb = pb[bank], pbuf[bank]
        S.op("pe", lambda h: h.transpose(pt[:, 0:128], src_ap, ident[:, :]),
             reads=(src_b, ident_b), writes=(ptb,))
        S.op("act", lambda h: h.activation(out=dsth[:, dcol:dcol + 128],
                                           in_=pt[:, 0:128], func=AF.Copy),
             reads=(ptb,), writes=(dsthb,))
        S.op("dve", lambda h: h.tensor_tensor(dstl[:, dcol:dcol + 128], pt[:, 0:128],
                                              dsth[:, dcol:dcol + 128],
                                              op=ALU.subtract),
             reads=(ptb, dsthb), writes=(dstlb,))

    def transpose_f32(src_ap, src_b, bank, dst_ap, dst_b):
        pt, ptb = pb[bank], pbuf[bank]
        S.op("pe", lambda h: h.transpose(pt[:, 0:128], src_ap, ident[:, :]),
             reads=(src_b, ident_b), writes=(ptb,))
        S.op("dve", lambda h: h.tensor_copy(out=dst_ap, in_=pt[:, 0:128]),
             reads=(ptb,), writes=(dst_b,))

    def mm3(bank_ap, bank_b, lh_ap, ll_ap, rh_ap, rl_ap, first, last, rbufs):
        S.op("pe", lambda h: h.matmul(bank_ap, lh_ap, rh_ap, start=first,
                                      stop=False),
             reads=rbufs, writes=(bank_b,))
        S.op("pe", lambda h: h.matmul(bank_ap, lh_ap, rl_ap, start=False,
                                      stop=False),
             reads=rbufs, writes=(bank_b,))
        S.op("pe", lambda h: h.matmul(bank_ap, ll_ap, rh_ap, start=False,
                                      stop=last),
             reads=rbufs, writes=(bank_b,))

    def dump_f32(idx, src_ap, src_b, ncols):
        ot, ob = x2o[0]
        done = 0
        while done < ncols:
            n = min(1024, ncols - done)
            S.op("dve", lambda h, done=done, n=n: h.tensor_copy(
                out=ot[:, 0:n], in_=src_ap[:, done:done + n]),
                reads=(src_b,), writes=(ob,))
            dma("gp", d_dbg[idx][:, done:done + n], ot[:, 0:n], (ob,), (), "dg")
            done += n

    astack = ExitStack()
    sba = mksb(astack)
    maskA, maskA_b = sba("maskA_s", [128, 2048], BF16)
    maskB, maskB_b = sba("maskB_s", [128, 1024], BF16)
    sel8, sel8_b = sba("sel8_s", [8, 512], F32)
    onesel, onesel_b = sba("onesel_s", [128, 64], BF16)
    wstage = [xt[0]]
    wA = [sba(f"wA{hl}", [128, 4096], BF16) for hl in range(2)]
    wB = [sba(f"wB{hl}", [128, 4096], BF16) for hl in range(2)]
    wpT32, wpT32_b = sba("wpT32", [128, 4096], F32)
    xnTh, xnTh_b = sba("xnTh", [128, 4096], BF16)
    xnTl, xnTl_b = sba("xnTl", [128, 4096], BF16)
    qT = [[sba(f"qT{hl}_{p}", [128, OWN], BF16) for p in range(4)]
          for hl in range(2)]
    kT = [[[sba(f"kT{hl}_{p}_{jp}", [128, 512], BF16) for jp in range(2)]
           for p in range(4)] for hl in range(2)]
    vv = [[sba(f"v{hl}_0", [128, 2048], BF16)] for hl in range(2)]
    p32 = [sba(f"p32_{i}", [128, 512], F32) for i in range(4)]
    ph = [sba(f"ph{i}", [128, 512], BF16) for i in range(4)]
    pl = [sba(f"pl{i}", [128, 512], BF16) for i in range(4)]
    acc = [sba(f"acc{p}", [128, OWN], F32) for p in range(4)]
    sums, sums_b = sba("sums", [8, OWN], F32)

    for half in range(2):
        _xt, _xtb = xt[0]
        dma("sp", _xt[:, :], d_maskA[:, 1024 * half:1024 * half + 1024],
            (), (_xtb,), "dx")
        S.op("dve", lambda h, _xt=_xt, half=half: h.tensor_copy(
            out=maskA[:, 1024 * half:1024 * half + 1024], in_=_xt[:, :]),
            reads=(_xtb,), writes=(maskA_b,))
    _xt, _xtb = xt[0]
    dma("sp", _xt[:, :], d_maskB[:, :], (), (_xtb,), "dx")
    S.op("dve", lambda h, _xt=_xt: h.tensor_copy(out=maskB[:, :], in_=_xt[:, :]),
         reads=(_xtb,), writes=(maskB_b,))
    dma("sp", sel8[0:8, :], d_sel8[:, :], (), (sel8_b,), "dx")
    _xt, _xtb = xt[0]
    dma("sp", _xt[:, 0:64], d_onesel[:, :], (), (_xtb,), "dx")
    S.op("dve", lambda h, _xt=_xt: h.tensor_copy(out=onesel[:, :],
                                                 in_=_xt[:, 0:64]),
         reads=(_xtb,), writes=(onesel_b,))

    def load_wT_split(dW, rows0, wdst):
        for r in range(4):
            ws, wsb = wstage[0]
            dma("sp", ws[:, :], dW[rows0 + 128 * r:rows0 + 128 * r + 128, :],
                (), (wsb,), "dx")
            for cc in range(NCH):
                bank = 4 + (cc % 2)
                transpose_bf(ws[:, 128 * cc:128 * cc + 128], wsb, bank,
                             wdst[0][0], wdst[0][1], wdst[1][0], wdst[1][1],
                             512 * cc + 128 * r)

    def xnT_tile(xnt, xnb, t):
        for ccg in range(2):
            bank = (2 * (t % 2) + ccg) % 4
            pt, ptb = pb[bank], pbuf[bank]
            for q in range(4):
                cc = 4 * ccg + q
                S.op("pe", lambda h, pt=pt, q=q, cc=cc, xnt=xnt: h.transpose(
                    pt[:, 128 * q:128 * q + 128],
                    xnt[:, 128 * cc:128 * cc + 128], ident[:, :]),
                    reads=(xnb, ident_b), writes=(ptb,))
            base = 512 * 4 * ccg + 128 * t
            oap_h = bass.AP(xnTh, base, [[4096, 128], [512, 4], [1, 128]])
            oap_l = bass.AP(xnTl, base, [[4096, 128], [512, 4], [1, 128]])
            iap = bass.AP(pb[bank], 0, [[512, 128], [128, 4], [1, 128]])
            S.op("act", lambda h, oap_h=oap_h, iap=iap: h.activation(
                out=oap_h, in_=iap, func=AF.Copy),
                reads=(ptb,), writes=(xnTh_b,))
            S.op("dve", lambda h, oap_l=oap_l, iap=iap, oap_h=oap_h:
                 h.tensor_tensor(oap_l, iap, oap_h, op=ALU.subtract),
                 reads=(ptb, xnTh_b), writes=(xnTl_b,))

    def gen_xnT(x_dram, row0, which_bufs=None):
        lnres = []
        for t in range(4):
            i = t % 2
            xtt, xtb = xt[i]
            dma("sp", xtt[:, :], x_dram[row0 + 128 * t:row0 + 128 * t + 128, :],
                (), (xtb,), "dx")
            lnres.append(ln_tile(i, xtt, xtb))
            if t >= 1:
                xnt, xnb = lnres[t - 1]
                xnT_tile(xnt, xnb, t - 1)
        xnt, xnb = lnres[3]
        xnT_tile(xnt, xnb, 3)

    def proj_pair(wT, dsth, dsthb, dstl, dstlb, dcol, pair, n, bank):
        bk, bkb = pb[bank], pbuf[bank]
        for cc in range(NCH):
            lh = wT[0][0][:, 512 * cc + 128 * pair:512 * cc + 128 * pair + 128]
            ll = wT[1][0][:, 512 * cc + 128 * pair:512 * cc + 128 * pair + 128]
            mm3(bk[:, 0:n], bkb, lh, ll, xnTh[:, 512 * cc:512 * cc + n],
                xnTl[:, 512 * cc:512 * cc + n],
                cc == 0, cc == NCH - 1, (wT[0][1], wT[1][1], xnTh_b, xnTl_b))
        S.op("act", lambda h: h.activation(out=dsth[:, dcol:dcol + n],
                                           in_=bk[:, 0:n], func=AF.Copy),
             reads=(bkb,), writes=(dsthb,))
        S.op("dve", lambda h: h.tensor_tensor(dstl[:, dcol:dcol + n], bk[:, 0:n],
                                              dsth[:, dcol:dcol + n],
                                              op=ALU.subtract),
             reads=(bkb, dsthb), writes=(dstlb,))

    ngroups = 2
    for g2 in range(ngroups):
        load_wT_split(d_wq, 512 * g2, wA)
        for s in range(4):
            gen_xnT(d_xq, 512 * s)
            if stage == "xnq" and g2 == 0 and s == 0:
                for cc in range(2):
                    dump_f32(2 * cc, xnT[0][cc][0][:, :], xnT[0][cc][1], 512)
                    dump_f32(2 * cc + 1, xnT[1][cc][0][:, :], xnT[1][cc][1], 512)
                break
            for p in range(4):
                proj_pair(wA, qT[0][p][0], qT[0][p][1], qT[1][p][0], qT[1][p][1],
                          512 * s, p, 512, 6 + (p % 2))
        if stage == "xnq":
            break
        if stage == "qkv" and g2 == 0:
            dump_f32(0, qT[0][0][0][:, :], qT[0][0][1], 2048)
            dump_f32(1, qT[1][0][0][:, :], qT[1][0][1], 2048)
            dump_f32(2, qT[0][3][0][:, :], qT[0][3][1], 2048)
            dump_f32(3, qT[1][3][0][:, :], qT[1][3][1], 2048)
            break

        load_wT_split(d_wk, 512 * g2, wA)
        load_wT_split(d_wv, 512 * g2, wB)

        for j in range(NKV):
            gen_xnT(d_xkv, 512 * j)
            for p in range(4):
                proj_pair(wA, kT[0][p][j % 2][0], kT[0][p][j % 2][1],
                          kT[1][p][j % 2][0], kT[1][p][j % 2][1],
                          0, p, 512, 6 + (p % 2))
            for m in range(4):
                bank = 6 + (m % 2)
                bk, bkb = pb[bank], pbuf[bank]
                for cc in range(NCH):
                    mm3(bk[:, :], bkb,
                        xnTh[:, 512 * cc + 128 * m:512 * cc + 128 * m + 128],
                        xnTl[:, 512 * cc + 128 * m:512 * cc + 128 * m + 128],
                        wB[0][0][:, 512 * cc:512 * cc + 512],
                        wB[1][0][:, 512 * cc:512 * cc + 512],
                        cc == 0, cc == NCH - 1,
                        (xnTh_b, xnTl_b, wB[0][1], wB[1][1]))
                vh_t, vh_b = vv[0][0]
                vl_t, vl_b = vv[1][0]
                S.op("act", lambda h, bk=bk, m=m, vh_t=vh_t: h.activation(
                    out=vh_t[:, 512 * m:512 * m + 512], in_=bk[:, :],
                    func=AF.Copy),
                    reads=(bkb,), writes=(vh_b,))
                S.op("dve", lambda h, bk=bk, m=m, vh_t=vh_t, vl_t=vl_t:
                     h.tensor_tensor(
                         vl_t[:, 512 * m:512 * m + 512], bk[:, :],
                         vh_t[:, 512 * m:512 * m + 512], op=ALU.subtract),
                     reads=(bkb, vh_b), writes=(vl_b,))

            for c in range(4):
                if j > 2 * c + 1:
                    continue
                if j == 2 * c + 1:
                    qt0, qn, msk = 512 * c + 256, 256, "B"
                elif j == 2 * c:
                    qt0, qn, msk = 512 * c, 512, "A"
                else:
                    qt0, qn, msk = 512 * c, 512, None
                smb, sm_bb = pb[6], pbuf[6]

                def mk_pre(p, m, qt0=None, qn=None, msk=None, j=None):
                    def emit_pre():
                        be, beb = pb[2 * (m % 2)], pbuf[2 * (m % 2)]
                        bo_, bob = pb[2 * (m % 2) + 1], pbuf[2 * (m % 2) + 1]
                        for (kw, qw, frst, lst) in ((0, 0, True, False),
                                                    (0, 1, False, False),
                                                    (1, 0, False, True)):
                            kh, khb = kT[kw][p][j % 2]
                            qh, qhb = qT[qw][p]
                            S.op("pe", lambda h, kh=kh, qh=qh, be=be,
                                 frst=frst, lst=lst: h.matmul(
                                     be[:, 0:qn],
                                     kh[0:64, 128 * m:128 * m + 128],
                                     qh[0:64, qt0:qt0 + qn],
                                     start=frst, stop=lst,
                                     tile_position=(0, 0)),
                                 reads=(khb, qhb), writes=(beb,))
                            S.op("pe", lambda h, kh=kh, qh=qh, bo_=bo_,
                                 frst=frst, lst=lst: h.matmul(
                                     bo_[:, 0:qn],
                                     kh[64:128, 128 * m:128 * m + 128],
                                     qh[64:128, qt0:qt0 + qn],
                                     start=frst, stop=lst,
                                     tile_position=(64, 0)),
                                 reads=(khb, qhb), writes=(bob,))
                        if msk == "A":
                            for bx, bxb in ((be, beb), (bo_, bob)):
                                S.op("dve", lambda h, bx=bx: h.tensor_tensor(
                                    bx[:, 0:512], bx[:, 0:512],
                                    maskA[:, 512 * m:512 * m + 512],
                                    op=ALU.add),
                                    reads=(bxb, maskA_b), writes=(bxb,))
                        elif msk == "B":
                            for bx, bxb in ((be, beb), (bo_, bob)):
                                S.op("dve", lambda h, bx=bx: h.tensor_tensor(
                                    bx[:, 0:256], bx[:, 0:256],
                                    maskB[:, 256 * m:256 * m + 256],
                                    op=ALU.add),
                                    reads=(bxb, maskB_b), writes=(bxb,))
                        for hh, (bx, bxb) in enumerate(((be, beb), (bo_, bob))):
                            pidx = 2 * (m % 2) + hh
                            p32t, p32b = p32[pidx]
                            pht, phb = ph[pidx]
                            plt, plb = pl[pidx]
                            S.op("act", lambda h, bx=bx, p32t=p32t:
                                 h.activation(out=p32t[:, 0:qn],
                                              in_=bx[:, 0:qn],
                                              func=AF.Exp, scale=QK_SCALE),
                                 reads=(bxb,), writes=(p32b,))
                            S.op("dve", lambda h, p32t=p32t, pht=pht:
                                 h.tensor_copy(out=pht[:, 0:qn],
                                               in_=p32t[:, 0:qn]),
                                 reads=(p32b,), writes=(phb,))
                            S.op("dve", lambda h, p32t=p32t, pht=pht, plt=plt:
                                 h.tensor_tensor(
                                     plt[:, 0:qn], p32t[:, 0:qn], pht[:, 0:qn],
                                     op=ALU.subtract),
                                 reads=(p32b, phb), writes=(plb,))
                    return emit_pre

                def mk_post(p, m, qt0=None, qn=None, j=None):
                    def emit_post():
                        for ti, (vw, pw) in enumerate(((0, 0), (0, 1), (1, 0))):
                            vt, vtb = vv[vw][0]
                            for hh in range(2):
                                pt, ptb = (ph if pw == 0
                                           else pl)[2 * (m % 2) + hh]
                                vcol = 512 * m + 128 * p + 64 * hh
                                S.op("pe", lambda h, vt=vt, pt=pt, vcol=vcol,
                                     hh=hh, bk=av_bk[hh],
                                     st=(m == 0 and ti == 0),
                                     sp=(m == 3 and ti == 2): h.matmul(
                                         bk[64 * hh:64 * hh + 64, 0:qn],
                                         vt[:, vcol:vcol + 64], pt[:, 0:qn],
                                         start=st, stop=sp,
                                         tile_position=(0, 64 * hh)),
                                     reads=(vtb, ptb), writes=(av_bb2[hh],))
                        for hh in range(2):
                            q8 = 2 * p + hh
                            for pw in range(2):
                                pt, ptb = (ph if pw == 0
                                           else pl)[2 * (m % 2) + hh]
                                S.op("pe", lambda h, q8=q8, pt=pt,
                                     st=(p == 0 and m == 0 and hh == 0
                                         and pw == 0),
                                     sp=(p == 3 and m == 3 and hh == 1
                                         and pw == 1):
                                     h.matmul(smb[0:8, 0:qn],
                                              onesel[:, 8 * q8:8 * q8 + 8],
                                              pt[:, 0:qn], start=st, stop=sp),
                                     reads=(onesel_b, ptb), writes=(sm_bb,))
                        if m == 3:
                            a_t, a_b = acc[p]
                            for hh in range(2):
                                r0, r1 = 64 * hh, 64 * hh + 64
                                bk = av_bk[hh]
                                if j == 0:
                                    S.op("dve", lambda h, a_t=a_t, bk=bk,
                                         r0=r0, r1=r1:
                                         h.tensor_copy(
                                             out=a_t[r0:r1, qt0:qt0 + qn],
                                             in_=bk[r0:r1, 0:qn]),
                                         reads=(av_bb2[hh],), writes=(a_b,))
                                else:
                                    S.op("dve", lambda h, a_t=a_t, bk=bk,
                                         r0=r0, r1=r1:
                                         h.tensor_tensor(
                                             a_t[r0:r1, qt0:qt0 + qn],
                                             bk[r0:r1, 0:qn],
                                             a_t[r0:r1, qt0:qt0 + qn],
                                             op=ALU.add),
                                         reads=(av_bb2[hh], a_b),
                                         writes=(a_b,))
                    return emit_post

                av_bk = (pb[4], pb[5])
                av_bb2 = (pbuf[4], pbuf[5])
                units = [(p, m) for p in range(4) for m in range(4)]
                pres = [mk_pre(p, m, qt0, qn, msk, j) for p, m in units]
                posts = [mk_post(p, m, qt0, qn, j) for p, m in units]
                pres[0]()
                for k in range(1, len(units)):
                    pres[k]()
                    posts[k - 1]()
                posts[-1]()
                if j == 0:
                    S.op("dve", lambda h, qt0=qt0, qn=qn: h.tensor_copy(
                        out=sums[0:8, qt0:qt0 + qn], in_=smb[0:8, 0:qn]),
                        reads=(sm_bb,), writes=(sums_b,))
                else:
                    S.op("dve", lambda h, qt0=qt0, qn=qn: h.tensor_tensor(
                        sums[0:8, qt0:qt0 + qn], smb[0:8, 0:qn],
                        sums[0:8, qt0:qt0 + qn], op=ALU.add),
                        reads=(sm_bb, sums_b), writes=(sums_b,))

        S.op("dve", lambda h: h.reciprocal(sums[0:8, :], sums[0:8, :]),
             reads=(sums_b,), writes=(sums_b,), drain_before=True, drain=True)
        for p in range(4):
            a_t, a_b = acc[p]
            for ch in range(4):
                bk, bkb = pb[6], pbuf[6]
                S.op("pe", lambda h, p=p, ch=ch, bk=bk: h.matmul(
                    bk[:, 0:512], sel8[0:8, 128 * p:128 * p + 128],
                    sums[0:8, 512 * ch:512 * ch + 512],
                    start=True, stop=True),
                    reads=(sel8_b, sums_b), writes=(bkb,))
                S.op("dve", lambda h, a_t=a_t, ch=ch, bk=bk: h.tensor_tensor(
                    a_t[:, 512 * ch:512 * ch + 512],
                    a_t[:, 512 * ch:512 * ch + 512], bk[:, 0:512], op=ALU.mult),
                    reads=(a_b, bkb), writes=(a_b,))
        if stage == "attn":
            for p in range(4):
                dump_f32(p, acc[p][0][:, :], acc[p][1], 2048)
            break

        for r in range(4):
            for bcol in range(NCH):
                ws, wsb = wstage[0]
                dma("sp", ws[:, 0:128],
                    d_wp[128 * bcol:128 * bcol + 128,
                         512 * g2 + 128 * r:512 * g2 + 128 * r + 128],
                    (), (wsb,), "dx")
                transpose_f32(ws[:, 0:128], wsb, 4 + ((r + bcol) % 2),
                              wpT32[:, 1024 * r + 128 * bcol:
                                    1024 * r + 128 * bcol + 128], wpT32_b)
        for tt in range(NT):
            i = tt % 2
            xtt, xtb = xt[i]
            if g2 == 0:
                dma("sp", xtt[:, :], d_xq[128 * tt:128 * tt + 128, :],
                    (), (xtb,), "dx")
            else:
                dma("sp", xtt[:, :], d_x2[128 * tt:128 * tt + 128, :],
                    (x2_bufs[tt],), (xtb,), "dx")
            ot, ob = x2o[0]
            for pn in range(2):
                bk, bkb = pb[6 + pn], pbuf[6 + pn]
                for p in range(4):
                    a_t, a_b = acc[p]
                    S.op("pe", lambda h, a_t=a_t, p=p, tt=tt, pn=pn, bk=bk:
                         h.matmul(bk[:, 0:512],
                                  a_t[:, 128 * tt:128 * tt + 128],
                                  wpT32[:, 1024 * p + 512 * pn:
                                        1024 * p + 512 * pn + 512],
                                  start=(p == 0), stop=(p == 3)),
                         reads=(a_b, wpT32_b), writes=(bkb,))
                S.op("dve", lambda h, ot=ot, pn=pn, bk=bk, xtt=xtt:
                     h.tensor_tensor(ot[:, 512 * pn:512 * pn + 512],
                                     bk[:, 0:512],
                                     xtt[:, 512 * pn:512 * pn + 512],
                                     op=ALU.add),
                     reads=(bkb, xtb), writes=(ob,))
            dma("gp", d_x2[128 * tt:128 * tt + 128, :], ot[:, :], (ob,),
                (x2_bufs[tt],), "dg")

    astack.close()

    if stage in ("full", "ffn1"):
        S.barrier()
        fstack = ExitStack()
        sbf = mksb(fstack)
        wm32, wm32_b = sbf("wm32", [128, 8192], F32)
        wos = [sbf("wos0", [128, 1024], F32)]
        emb32, emb32_b = sbf("emb32", [128, NEXP], F32)
        sel16, sel16_b = sbf("sel16_s", [16, 2048], BF16)
        b2T, b2T_b = sbf("b2T", [16, 128], F32)
        xn2T = [sbf(f"xn2T{cc}", [128, 512], F32) for cc in range(NCH)]
        hT32, hT32_b = sbf("hT32", [128, 4096], F32)
        tTh, tTh_b = sbf("tTh", [128, 4096], BF16)
        tTl, tTl_b = sbf("tTl", [128, 4096], BF16)
        lg, lg_b = sbf("lg", [128, 128], F32)
        ew4, ew4_b = sbf("ew4", [128, 64], F32)
        aux4, aux4_b = sbf("aux4", [128, 64], F32)
        nm4, nm4_b = sbf("nm4", [128, 4], F32)
        sc4, sc4_b = sbf("sc4", [128, 4], F32)
        m14, m14_b = sbf("m14", [128, 4], F32)
        cmat4 = [sbf(f"cmat{i}", [128, 128], F32) for i in range(2)]
        cTh = [sbf(f"cTh{i}", [16, 4096], BF16) for i in range(2)]
        cTl = [sbf(f"cTl{i}", [16, 4096], BF16) for i in range(2)]
        b2h, b2h_b = sbf("b2h", [16, 128], BF16)
        b2l, b2l_b = sbf("b2l", [16, 128], BF16)
        w1s = [sbf(f"w1s{i}", [128, FI], F32) for i in range(2)]
        w1h = [sbf(f"w1h{i}", [128, FI], BF16) for i in range(2)]
        w1l = [sbf(f"w1l{i}", [128, FI], BF16) for i in range(2)]
        w2s = [sbf(f"w2s{i}", [128, FI], F32) for i in range(2)]
        b1s = [sbf(f"b1s{i}", [128, 4], F32) for i in range(2)]
        bms, bms_b = sbf("bms", [128, 8], F32)
        bos, bos_b = sbf("bos", [1, D], F32)
        bobc, bobc_b = sbf("bobc", [128, D], F32)
        onesr, onesr_b = sbf("onesr", [1, 128], F32)
        hmid = [sbf(f"hmid{i}", [128, 2048], F32) for i in range(2)]
        comb, comb_b = sbf("comb", [128, 4096], F32)
        cbc, cbc_b = sbf("cbc", [128, 512], F32)
        tmp1, tmp1_b = sbf("tmp1", [128, 512], F32)

        dma("sp", bms[:, :], d_bm[:, :], (), (bms_b,), "dx")
        dma("sp", bos[0:1, :], d_bo[0:1, :], (), (bos_b,), "dx")
        dma("sp", emb32[:, :], d_emb[:, :], (), (emb32_b,), "dx")
        _st, _stb = hmid[0]
        dma("sp", _st[0:16, :], d_sel16[:, :], (), (_stb,), "dx")
        S.op("dve", lambda h, _st=_st: h.tensor_copy(out=sel16[0:16, :],
                                                     in_=_st[0:16, :]),
             reads=(_stb,), writes=(sel16_b,))
        dma("sp", b2T[0:16, :], d_b2[:, :], (), (b2T_b,), "dx")
        S.op("dve", lambda h: h.tensor_copy(out=b2h[0:16, :], in_=b2T[0:16, :]),
             reads=(b2T_b,), writes=(b2h_b,))
        S.op("dve", lambda h: h.tensor_tensor(b2l[0:16, :], b2T[0:16, :],
                                              b2h[0:16, :], op=ALU.subtract),
             reads=(b2T_b, b2h_b), writes=(b2l_b,))
        for r in range(NCH):
            dma("sp", wm32[:, 1024 * r:1024 * r + 1024],
                d_wm[128 * r:128 * r + 128, :], (), (wm32_b,), "dx")
        S.op("dve", lambda h: h.memset(onesr[0:1, :], 1.0), reads=(),
             writes=(onesr_b,))
        for chn in range(2):
            bk, bkb = pb[6], pbuf[6]
            S.op("pe", lambda h, chn=chn, bk=bk: h.matmul(
                bk[:, 0:512], onesr[0:1, :], bos[0:1, 512 * chn:512 * chn + 512],
                start=True, stop=True),
                reads=(onesr_b, bos_b), writes=(bkb,))
            S.op("dve", lambda h, chn=chn, bk=bk: h.tensor_copy(
                out=bobc[:, 512 * chn:512 * chn + 512], in_=bk[:, 0:512]),
                reads=(bkb,), writes=(bobc_b,))

        def prologue_chunks(sx):
            par = sx % 2
            cth_t, cth_b = cTh[par]
            ctl_t, ctl_b = cTl[par]
            chunks = []

            def mk_tile(t):
                def em():
                    x2tt, x2tb = xt[t % 2]
                    dma("sp", x2tt[:, :],
                        d_x2[512 * sx + 128 * t:512 * sx + 128 * t + 128, :],
                        (x2_bufs[4 * sx + t],), (x2tb,), "dx")
                    xnt, xnb = ln_tile(t % 2, x2tt, x2tb)
                    for cc in range(NCH):
                        dt_t, dt_b = xn2T[cc]
                        transpose_f32(xnt[:, 128 * cc:128 * cc + 128], xnb,
                                      cc % 4, dt_t[:, 128 * t:128 * t + 128],
                                      dt_b)
                return em

            def mk_wm(h0):
                def em():
                    for hc in (h0, h0 + 1):
                        bk, bkb = pb[6 + (hc % 2)], pbuf[6 + (hc % 2)]
                        for cc in range(NCH):
                            xt_t, xt_b2 = xn2T[cc]
                            S.op("pe", lambda h, hc=hc, cc=cc, bk=bk,
                                 xt_t=xt_t: h.matmul(
                                     bk[:, 0:512],
                                     wm32[:, 1024 * cc + 128 * hc:
                                          1024 * cc + 128 * hc + 128],
                                     xt_t[:, 0:512], start=(cc == 0),
                                     stop=(cc == NCH - 1)),
                                 reads=(wm32_b, xt_b2), writes=(bkb,))
                        S.op("dve", lambda h, hc=hc, bk=bk: h.tensor_scalar(
                            hT32[:, 512 * hc:512 * hc + 512], bk[:, 0:512],
                            bms[:, hc:hc + 1], None, op0=ALU.add),
                            reads=(bkb, bms_b), writes=(hT32_b,))
                return em

            def mk_router(tm):
                cmat, cmat_b = cmat4[tm % 2]
                def em():
                    bk, bkb = pb[tm % 2], pbuf[tm % 2]
                    for jj in range(NJ):
                        S.op("pe", lambda h, jj=jj, bk=bk: h.matmul(
                            bk[:, 16 * jj:16 * jj + 16],
                            hT32[:, 512 * jj + 128 * tm:
                                 512 * jj + 128 * tm + 128],
                            emb32[:, :], start=True, stop=True),
                            reads=(hT32_b, emb32_b), writes=(bkb,))
                    S.op("dve", lambda h, bk=bk: h.tensor_copy(
                        out=lg[:, :], in_=bk[:, 0:128]),
                        reads=(bkb,), writes=(lg_b,))
                    for jb in range(0, NJ, 4):
                        def sl(q, t16):
                            return t16[:, 16 * q:16 * q + 16]
                        for q in range(4):
                            S.op("dve", lambda h, q=q, jj=jb + q: h.reduce_max(
                                nm4[:, q:q + 1], lg[:, 16 * jj:16 * jj + 16],
                                axis=AX.X, negate=True),
                                reads=(lg_b,), writes=(nm4_b,),
                                drain_before=(q == 0), drain=(q == 3))
                        for q in range(4):
                            S.op("act", lambda h, q=q, jj=jb + q: h.activation(
                                out=sl(q, ew4), in_=lg[:, 16 * jj:16 * jj + 16],
                                func=AF.Exp, bias=nm4[:, q:q + 1]),
                                reads=(lg_b, nm4_b), writes=(ew4_b,))
                        for q in range(4):
                            S.op("dve", lambda h, q=q: h.reduce_sum(
                                sc4[:, q:q + 1], sl(q, ew4), axis=AX.X),
                                reads=(ew4_b,), writes=(sc4_b,),
                                drain_before=(q == 0), drain=(q == 3))
                        for q in range(4):
                            S.op("dve", lambda h, q=q: h.reciprocal(
                                sc4[:, q:q + 1], sc4[:, q:q + 1]),
                                reads=(sc4_b,), writes=(sc4_b,),
                                drain_before=(q == 0), drain=(q == 3))
                        for q in range(4):
                            S.op("dve", lambda h, q=q: h.reduce_max(
                                m14[:, q:q + 1], sl(q, ew4), axis=AX.X),
                                reads=(ew4_b,), writes=(m14_b,),
                                drain_before=(q == 0), drain=(q == 3))
                        for q in range(4):
                            S.op("dve", lambda h, q=q: h.tensor_scalar(
                                sl(q, aux4), sl(q, ew4), m14[:, q:q + 1], None,
                                op0=ALU.is_equal),
                                reads=(ew4_b, m14_b), writes=(aux4_b,),
                                drain_before=(q == 0))
                        for q in range(4):
                            S.op("dve", lambda h, q=q: h.scalar_tensor_tensor(
                                sl(q, aux4), sl(q, aux4), -2.0, sl(q, ew4),
                                op0=ALU.mult, op1=ALU.add),
                                reads=(aux4_b, ew4_b), writes=(aux4_b,),
                                drain_before=(q == 0))
                        for q in range(4):
                            S.op("dve", lambda h, q=q: h.reduce_max(
                                m14[:, q:q + 1], sl(q, aux4), axis=AX.X),
                                reads=(aux4_b,), writes=(m14_b,),
                                drain_before=(q == 0), drain=(q == 3))
                        for q in range(4):
                            S.op("dve", lambda h, q=q: h.tensor_scalar(
                                sl(q, aux4), sl(q, ew4), m14[:, q:q + 1], None,
                                op0=ALU.is_ge),
                                reads=(ew4_b, m14_b), writes=(aux4_b,),
                                drain_before=(q == 0))
                        for q in range(4):
                            S.op("dve", lambda h, q=q, jj=jb + q:
                                 h.scalar_tensor_tensor(
                                     cmat[:, 16 * jj:16 * jj + 16], sl(q, ew4),
                                     sc4[:, q:q + 1], sl(q, aux4),
                                     op0=ALU.mult, op1=ALU.mult),
                                 reads=(ew4_b, sc4_b, aux4_b), writes=(cmat_b,),
                                 drain_before=(q == 0))
                return em

            def mk_ctrans(tm, cm_t, cm_b):
                def em():
                    for jj in range(NJ):
                        bk2, bk2b = pb[2 + (jj % 2)], pbuf[2 + (jj % 2)]
                        S.op("pe", lambda h, jj=jj, bk2=bk2, cm_t=cm_t:
                             h.transpose(
                                 bk2[0:16, 0:128],
                                 cm_t[:, 16 * jj:16 * jj + 16], ident[:, :]),
                             reads=(cm_b, ident_b), writes=(bk2b,))
                        col = 512 * jj + 128 * tm
                        S.op("act", lambda h, bk2=bk2, col=col: h.activation(
                            out=cth_t[0:16, col:col + 128],
                            in_=bk2[0:16, 0:128], func=AF.Copy),
                            reads=(bk2b,), writes=(cth_b,))
                        S.op("dve", lambda h, bk2=bk2, col=col: h.tensor_tensor(
                            ctl_t[0:16, col:col + 128], bk2[0:16, 0:128],
                            cth_t[0:16, col:col + 128], op=ALU.subtract),
                            reads=(bk2b, cth_b), writes=(ctl_b,))
                return em

            for t in range(4):
                chunks.append(mk_tile(t))
            for h0 in (0, 2, 4, 6):
                chunks.append(mk_wm(h0))
            rpend = []
            for tm in range(4):
                chunks.append(mk_router(tm))
                rpend.append(mk_ctrans(tm, *cmat4[tm % 2]))
                if tm >= 1:
                    chunks.append(rpend[tm - 1])
            chunks.append(rpend[3])
            return chunks

        def tT_split(sx):
            for hc in range(NCH):
                S.op("act", lambda h, hc=hc: h.activation(
                    out=tTh[:, 512 * hc:512 * hc + 512],
                    in_=hT32[:, 512 * hc:512 * hc + 512], func=AF.Copy),
                    reads=(hT32_b,), writes=(tTh_b,))
                S.op("dve", lambda h, hc=hc: h.tensor_tensor(
                    tTl[:, 512 * hc:512 * hc + 512],
                    hT32[:, 512 * hc:512 * hc + 512],
                    tTh[:, 512 * hc:512 * hc + 512], op=ALU.subtract),
                    reads=(hT32_b, tTh_b), writes=(tTl_b,))

        for ch in prologue_chunks(0):
            ch()
        tT_split(0)
        for s in range(4):
            par = s % 2
            cth_t, cth_b = cTh[par]
            ctl_t, ctl_b = cTl[par]
            nxt = prologue_chunks(s + 1) if s < 3 else []
            for e in range(NEXP):
                w1st, w1sb = w1s[e % 2]
                w1ht, w1hb = w1h[e % 2]
                w1lt, w1lb = w1l[e % 2]
                w2t, w2b = w2s[e % 2]
                b1t, b1b = b1s[e % 2]
                dma("sp", w1st[:, :], d_w1[e, :, :], (), (w1sb,), "dx")
                S.op("dve", lambda h, w1st=w1st, w1ht=w1ht: h.tensor_copy(
                    out=w1ht[:, :], in_=w1st[:, :]),
                    reads=(w1sb,), writes=(w1hb,))
                S.op("dve", lambda h, w1st=w1st, w1ht=w1ht, w1lt=w1lt:
                     h.tensor_tensor(w1lt[:, :], w1st[:, :], w1ht[:, :],
                                     op=ALU.subtract),
                     reads=(w1sb, w1hb), writes=(w1lb,))
                dma("sp", w2t[:, :], d_w2r[e, :, :], (), (w2b,), "dx")
                dma("sp", b1t[:, :], d_b1[e, :, :], (), (b1b,), "dx")
                for jj in range(NJ):
                    hm_t, hm_b = hmid[jj % 2]
                    for ic in range(4):
                        bk, bkb = pb[ic % 2], pbuf[ic % 2]
                        mm3(bk[:, 0:512], bkb,
                            w1ht[:, 128 * ic:128 * ic + 128],
                            w1lt[:, 128 * ic:128 * ic + 128],
                            tTh[:, 512 * jj:512 * jj + 512],
                            tTl[:, 512 * jj:512 * jj + 512],
                            True, True, (w1hb, w1lb, tTh_b, tTl_b))
                        S.op("act", lambda h, ic=ic, bk=bk, hm_t=hm_t, b1t=b1t:
                             h.activation(out=hm_t[:, 512 * ic:512 * ic + 512],
                                          in_=bk[:, 0:512], func=AF.Gelu,
                                          bias=b1t[:, ic:ic + 1]),
                             reads=(bkb, b1b), writes=(hm_b,))
                    bke, bkeb = pb[2 + (jj % 2)], pbuf[2 + (jj % 2)]
                    for ic in range(4):
                        S.op("pe", lambda h, ic=ic, bke=bke, w2t=w2t, hm_t=hm_t:
                             h.matmul(bke[:, 0:512],
                                      w2t[:, 128 * ic:128 * ic + 128],
                                      hm_t[:, 512 * ic:512 * ic + 512],
                                      start=(ic == 0), stop=(ic == 3)),
                             reads=(w2b, hm_b), writes=(bkeb,))
                    bkc, bkcb = pb[4], pbuf[4]
                    S.op("pe", lambda h, e=e, jj=jj, bkc=bkc, cth_t=cth_t:
                         h.matmul(
                             bkc[:, 0:512], sel16[0:16, 128 * e:128 * e + 128],
                             cth_t[0:16, 512 * jj:512 * jj + 512], start=True,
                             stop=False),
                         reads=(sel16_b, cth_b), writes=(bkcb,))
                    S.op("pe", lambda h, e=e, jj=jj, bkc=bkc, ctl_t=ctl_t:
                         h.matmul(
                             bkc[:, 0:512], sel16[0:16, 128 * e:128 * e + 128],
                             ctl_t[0:16, 512 * jj:512 * jj + 512], start=False,
                             stop=True),
                         reads=(sel16_b, ctl_b), writes=(bkcb,))
                    S.op("act", lambda h, bkc=bkc: h.activation(
                        out=cbc[:, 0:512], in_=bkc[:, 0:512], func=AF.Copy),
                        reads=(bkcb,), writes=(cbc_b,))
                    if e == 0:
                        S.op("dve", lambda h, jj=jj, bke=bke: h.tensor_tensor(
                            comb[:, 512 * jj:512 * jj + 512], bke[:, 0:512],
                            cbc[:, 0:512], op=ALU.mult),
                            reads=(bkeb, cbc_b), writes=(comb_b,))
                    else:
                        S.op("dve", lambda h, bke=bke: h.tensor_tensor(
                            tmp1[:, 0:512], bke[:, 0:512], cbc[:, 0:512],
                            op=ALU.mult),
                            reads=(bkeb, cbc_b), writes=(tmp1_b,))
                        S.op("dve", lambda h, jj=jj: h.tensor_tensor(
                            comb[:, 512 * jj:512 * jj + 512], tmp1[:, 0:512],
                            comb[:, 512 * jj:512 * jj + 512], op=ALU.add),
                            reads=(tmp1_b, comb_b), writes=(comb_b,))
                if nxt and 1 <= e <= len(nxt):
                    nxt[e - 1]()
            for k in range(NEXP - 1, len(nxt)):
                nxt[k]()
            for jj in range(NJ):
                bk, bkb = pb[5], pbuf[5]
                mm3(bk[:, 0:512], bkb, b2h[0:16, :], b2l[0:16, :],
                    cth_t[0:16, 512 * jj:512 * jj + 512],
                    ctl_t[0:16, 512 * jj:512 * jj + 512],
                    True, True, (b2h_b, b2l_b, cth_b, ctl_b))
                S.op("dve", lambda h, jj=jj, bk=bk: h.tensor_tensor(
                    comb[:, 512 * jj:512 * jj + 512], bk[:, 0:512],
                    comb[:, 512 * jj:512 * jj + 512], op=ALU.add),
                    reads=(bkb, comb_b), writes=(comb_b,))
            for dc in range(NCH):
                wot, wob = wos[0]
                dma("sp", wot[:, :], d_wo[128 * dc:128 * dc + 128, :],
                    (), (wob,), "dx")
                for tm in range(4):
                    for pn in range(2):
                        bk, bkb = pb[2 * tm + pn], pbuf[2 * tm + pn]
                        S.op("pe", lambda h, dc=dc, tm=tm, pn=pn, bk=bk,
                             wot=wot: h.matmul(
                                 bk[:, 0:512],
                                 comb[:, 512 * dc + 128 * tm:
                                      512 * dc + 128 * tm + 128],
                                 wot[:, 512 * pn:512 * pn + 512],
                                 start=(dc == 0), stop=(dc == NCH - 1)),
                             reads=(comb_b, wob), writes=(bkb,))
            for tm in range(4):
                x2tt, x2tb = xt[tm % 2]
                dma("sp", x2tt[:, :],
                    d_x2[512 * s + 128 * tm:512 * s + 128 * tm + 128, :],
                    (x2_bufs[4 * s + tm],), (x2tb,), "dx")
                ot, ob = x2o[0]
                for pn in range(2):
                    bk, bkb = pb[2 * tm + pn], pbuf[2 * tm + pn]
                    S.op("dve", lambda h, pn=pn, bk=bk: h.tensor_tensor(
                        tmp1[:, 0:512], bk[:, 0:512],
                        bobc[:, 512 * pn:512 * pn + 512], op=ALU.add),
                        reads=(bkb, bobc_b), writes=(tmp1_b,))
                    S.op("dve", lambda h, ot=ot, pn=pn, x2tt=x2tt:
                         h.tensor_tensor(ot[:, 512 * pn:512 * pn + 512],
                                         tmp1[:, 0:512],
                                         x2tt[:, 512 * pn:512 * pn + 512],
                                         op=ALU.add),
                         reads=(tmp1_b, x2tb), writes=(ob,))
                dma("gp", d_out[512 * s + 128 * tm:512 * s + 128 * tm + 128, :],
                    ot[:, :], (ob,), (), "dg")
            if s < 3:
                tT_split(s + 1)
        fstack.close()

    stack0.close()
    S.final_wait("gp", "dg")

    sems = {}
    for e in ENGINES:
        sems[e] = nc.alloc_semaphore("sem_" + e)
    for k in S.dcnt:
        sems[k] = nc.alloc_semaphore("sem_" + k[2:])
    S.prepare()
    with nc.Block() as block:
        @block.tensor
        def _(t):
            S.emit_engine("pe", t, sems)

        @block.scalar
        def _(sc):
            S.emit_engine("act", sc, sems)

        @block.vector
        def _(v):
            S.emit_engine("dve", v, sems)

        @block.gpsimd
        def _(gp_):
            S.emit_engine("gp", gp_, sems)

        @block.sync
        def _(sy):
            S.emit_engine("sp", sy, sems)
    return nc


def _host_inputs(inputs, core):
    b, g = core // 2, core % 2
    x_b = np.ascontiguousarray(np.asarray(inputs["L0_attn_x"])[b])
    x_q = np.ascontiguousarray(x_b.reshape(32, 128, D)[g::2].reshape(OWN, D))
    maskA, maskB = _causal_masks(g)
    sel8 = np.zeros((8, 512), dtype=np.float32)
    for p in range(4):
        sel8[2 * p, 128 * p:128 * p + 64] = 1.0 / 1024.0
        sel8[2 * p + 1, 128 * p + 64:128 * p + 128] = 1.0 / 1024.0
    onesel = np.zeros((128, 64), dtype=np.float32)
    for q in range(8):
        onesel[:, 8 * q + q] = 1.0
    sel16 = np.zeros((16, 2048), dtype=np.float32)
    for e in range(16):
        sel16[e, 128 * e:128 * e + 128] = 1.0
    b1 = np.asarray(inputs["L0_ffn_b1"]).reshape(NEXP, 4, 128).transpose(0, 2, 1)
    bm = np.asarray(inputs["L0_ffn_bm"]).reshape(8, 128).T
    return {
        "x_q": x_q,
        "x_kv": x_b,
        "w_q": np.asarray(inputs["L0_attn_W_query"]),
        "w_k": np.asarray(inputs["L0_attn_W_key"]),
        "w_v": np.asarray(inputs["L0_attn_W_value"]),
        "w_p": np.asarray(inputs["L0_attn_W_proj"]),
        "w_m": np.asarray(inputs["L0_ffn_Wm"]),
        "w_o": np.asarray(inputs["L0_ffn_Wo"]),
        "w_1": np.asarray(inputs["L0_ffn_W1"]),
        "w_2": np.ascontiguousarray(np.asarray(inputs["L0_ffn_W2"]).reshape(NEXP, 4, 128, 128).transpose(0, 2, 1, 3).reshape(NEXP, 128, FI)),
        "b_1": np.ascontiguousarray(b1),
        "b_2": np.asarray(inputs["L0_ffn_b2"]),
        "b_m": np.ascontiguousarray(bm),
        "b_o": np.asarray(inputs["L0_ffn_bo"]).reshape(1, D),
        "emb": np.asarray(inputs["L0_ffn_expert_emb"]),
        "maskA": maskA,
        "maskB": maskB,
        "ident": np.eye(128, dtype=np.float32),
        "sel8": sel8,
        "onesel": onesel,
        "sel16": sel16,
    }


def kernel(**inputs):
    nc = build_nc("full")
    in_maps = [_host_inputs(inputs, core) for core in range(8)]
    res = run_bass_kernel_spmd(nc, in_maps, core_ids=list(range(8)))
    y = np.empty((B, T, D), dtype=np.float32)
    for core in range(8):
        b, g = core // 2, core % 2
        out = np.asarray(res.results[core]["out"])
        y[b].reshape(32, 128, D)[g::2] = out.reshape(NT, 128, D)
    return y


# version 27
# speedup vs baseline: 1.0508x; 1.0508x over previous
from contextlib import ExitStack

import numpy as np
import ml_dtypes

import concourse.bass as bass
import concourse.mybir as mybir
from concourse.bass_utils import run_bass_kernel_spmd

F32 = mybir.dt.float32
BF16 = mybir.dt.bfloat16
AF = mybir.ActivationFunctionType
ALU = mybir.AluOpType
AX = mybir.AxisListType

B, T, D = 4, 4096, 1024
OWN = 2048
NT = 16
NKV = 8
NCH = 8
NEXP, FI = 16, 512
NJ = 8
EPS = 1e-5
QK_SCALE = 1.0 / 8192.0
AV_SCALE = 1.0 / 1024.0

ENGINES = ("pe", "act", "dve", "gp", "sp")


class Buf:
    __slots__ = ("name", "writer", "readers")

    def __init__(self, name):
        self.name = name
        self.writer = None
        self.readers = {}


class Sched:

    DPOOL = {"dx": 48, "dg": 12}

    def __init__(self):
        self.ops = []
        self.ecnt = {e: 0 for e in ENGINES}
        self.dcnt = {}
        self.didx = {}

    def op(self, eng, emit, reads=(), writes=(), dma=None, drain=False,
           drain_before=False):
        waits = {}

        def need(tok):
            if tok is None:
                return
            s, v = tok
            if s == eng and eng == "pe":
                return
            if waits.get(s, 0) < v:
                waits[s] = v

        for b in reads:
            need(b.writer)
        for b in writes:
            need(b.writer)
            for t in b.readers.items():
                need(t)
        if dma is None:
            self.ecnt[eng] += 1
            tok = (eng, self.ecnt[eng])
        else:
            i = self.didx.get(dma, 0)
            self.didx[dma] = (i + 1) % self.DPOOL[dma]
            key = f"d:{dma}{i}"
            prev = self.dcnt.get(key, 0)
            if prev:
                if waits.get(key, 0) < prev:
                    waits[key] = prev
            self.dcnt[key] = prev + 16
            tok = (key, self.dcnt[key])
        for b in reads:
            if b.readers.get(tok[0], 0) < tok[1]:
                b.readers[tok[0]] = tok[1]
        for b in writes:
            b.writer = tok
            b.readers = {}
        self.ops.append((eng, emit, waits, tok, dma,
                         (drain_before, drain)))
        return tok

    def barrier(self):
        toks = [(e, self.ecnt[e]) for e in ENGINES if self.ecnt[e] > 0]
        toks += list(self.dcnt.items())
        for e in ENGINES:
            waits = {s: v for s, v in toks if s != e}
            self.ecnt[e] += 1
            self.ops.append((e, None, waits, (e, self.ecnt[e]), None,
                             (False, False)))

    def final_wait(self, eng, queue):
        waits = {k: v for k, v in self.dcnt.items()
                 if k.startswith("d:" + queue) and v}
        if waits:
            self.ecnt[eng] += 1
            self.ops.append((eng, None, waits, (eng, self.ecnt[eng]), None,
                             (False, False)))

    def prepare(self):
        needed = set()
        for o in self.ops:
            needed.update(o[2].items())
        incmap = {}
        for e in ENGINES:
            vals = sorted(v for (s, v) in needed if s == e)
            for newv, origv in enumerate(vals, 1):
                incmap[(e, origv)] = newv
        self._needed = needed
        self._incmap = incmap

    def emit_engine(self, e, h, sems):
        dw = {}
        for _e, emit, waits, tok, dma, drain in self.ops:
            if _e != e:
                continue
            for s in sorted(waits):
                v = waits[s]
                if not s.startswith("d:"):
                    v = self._incmap[(s, v)]
                if dw.get(s, 0) >= v:
                    continue
                dw[s] = v
                h.wait_ge(sems[s], v)
            if emit is None:
                continue
            if drain[0]:
                h.drain()
            inst = emit(h)
            if drain[1]:
                d = h.drain()
                if d is not None:
                    inst = d
            if dma is not None:
                inst.then_inc(sems[tok[0]], 16)
            elif (e, tok[1]) in self._needed:
                inst.then_inc(sems[e], 1)


def _causal_masks(g):
    p = np.arange(128)[:, None, None, None]
    m = np.arange(4)[None, :, None, None]
    i4 = np.arange(4)[None, None, :, None]
    ql = np.arange(128)[None, None, None, :]
    okA = (128 * m + p) <= (256 * i4 + 128 * g + ql)
    i2 = np.arange(2, 4)[None, None, :, None]
    okB = (512 + 128 * m + p) <= (256 * i2 + 128 * g + ql)
    maskA = np.where(okA, 0.0, -1e30).astype(np.float32).reshape(128, 2048)
    maskB = np.where(okB, 0.0, -1e30).astype(np.float32).reshape(128, 1024)
    return maskA, maskB


def build_nc(stage="full"):
    nc = bass.Bass("TRN2", target_bir_lowering=False)

    d_xq = nc.dram_tensor("x_q", [OWN, D], F32, kind="ExternalInput").ap()
    d_xkv = nc.dram_tensor("x_kv", [T, D], F32, kind="ExternalInput").ap()
    d_wq = nc.dram_tensor("w_q", [D, D], F32, kind="ExternalInput").ap()
    d_wk = nc.dram_tensor("w_k", [D, D], F32, kind="ExternalInput").ap()
    d_wv = nc.dram_tensor("w_v", [D, D], F32, kind="ExternalInput").ap()
    d_wp = nc.dram_tensor("w_p", [D, D], F32, kind="ExternalInput").ap()
    d_wm = nc.dram_tensor("w_m", [D, D], F32, kind="ExternalInput").ap()
    d_wo = nc.dram_tensor("w_o", [D, D], F32, kind="ExternalInput").ap()
    d_w1 = nc.dram_tensor("w_1", [NEXP, 128, FI], F32, kind="ExternalInput").ap()
    d_w2r = nc.dram_tensor("w_2", [NEXP, 128, FI], F32,
                           kind="ExternalInput").ap()
    d_b1 = nc.dram_tensor("b_1", [NEXP, 128, 4], F32, kind="ExternalInput").ap()
    d_b2 = nc.dram_tensor("b_2", [NEXP, 128], F32, kind="ExternalInput").ap()
    d_bm = nc.dram_tensor("b_m", [128, 8], F32, kind="ExternalInput").ap()
    d_bo = nc.dram_tensor("b_o", [1, D], F32, kind="ExternalInput").ap()
    d_emb = nc.dram_tensor("emb", [128, NEXP], F32, kind="ExternalInput").ap()
    d_maskA = nc.dram_tensor("maskA", [128, 2048], F32, kind="ExternalInput").ap()
    d_maskB = nc.dram_tensor("maskB", [128, 1024], F32, kind="ExternalInput").ap()
    d_ident = nc.dram_tensor("ident", [128, 128], F32, kind="ExternalInput").ap()
    d_sel8 = nc.dram_tensor("sel8", [8, 512], F32, kind="ExternalInput").ap()
    d_onesel = nc.dram_tensor("onesel", [128, 64], F32, kind="ExternalInput").ap()
    d_sel16 = nc.dram_tensor("sel16", [16, 2048], F32, kind="ExternalInput").ap()
    d_out = nc.dram_tensor("out", [OWN, D], F32, kind="ExternalOutput").ap()
    x2kind = {"kind": "ExternalOutput"} if stage == "oproj" else {}
    d_x2 = nc.dram_tensor("x2scratch", [OWN, D], F32, **x2kind).ap()
    x2_bufs = [Buf(f"x2d{t}") for t in range(NT)]
    x2_half = [[Buf(f"x2d{t}h{p}") for p in range(2)] for t in range(NT)]

    ndbg = 4 if stage in ("xnq", "qkv", "attn", "ffn1") else 0
    d_dbg = [nc.dram_tensor(f"dbg{i}", [128, 2048], F32, kind="ExternalOutput").ap()
             for i in range(ndbg)]

    S = Sched()

    pb = [nc.alloc_psum_tensor(f"pb{i}", [128, 512], F32) for i in range(8)]
    pbuf = [Buf(f"pb{i}") for i in range(8)]

    stack0 = ExitStack()

    def mksb(stk):
        def sba(name, shape, dt):
            t = stk.enter_context(nc.sbuf_tensor(name, shape, dt))
            return t, Buf(name)
        return sba

    sb0 = mksb(stack0)

    ident, ident_b = sb0("identity", [128, 128], F32)
    xt = [sb0(f"xt{i}", [128, D], F32) for i in range(2)]
    xn = [sb0(f"xn{i}", [128, D], F32) for i in range(2)]
    stat6 = [sb0(f"stat6_{i}", [128, 12], F32) for i in range(2)]
    mvs = [sb0(f"mv{i}", [128, 2], F32) for i in range(2)]
    negm = [sb0(f"negm{i}", [128, 1], F32) for i in range(2)]
    rstd = [sb0(f"rstd{i}", [128, 1], F32) for i in range(2)]
    x2o = [sb0("x2o0", [128, D], F32)]
    epsc, epsc_b = sb0("epsc", [128, 1], F32)

    def dma(engq, dst_ap, src_ap, reads, writes, q):
        S.op(engq, lambda h, d=dst_ap, s=src_ap: h.dma_start(out=d, in_=s),
             reads=reads, writes=writes, dma=q)

    dma("sp", ident[:, :], d_ident[:, :], (), (ident_b,), "dx")
    S.op("dve", lambda h: h.memset(epsc[:, 0:1], EPS), reads=(), writes=(epsc_b,))

    def ln_tile(i, xt_ap, xt_b):
        s6, s6b = stat6[i]
        mvt, mvb = mvs[i]
        nm, nmb = negm[i]
        rs, rsb = rstd[i]
        xnt, xnb = xn[i % 2]
        S.op("dve", lambda h: h.bn_stats(s6[:, 0:6], xt_ap[:, 0:512]),
             reads=(xt_b,), writes=(s6b,), drain_before=True)
        S.op("dve", lambda h: h.bn_stats(s6[:, 6:12], xt_ap[:, 512:1024]),
             reads=(xt_b,), writes=(s6b,), drain=True)
        S.op("dve", lambda h: h.bn_aggr(mvt[:, 0:2], s6[:, 0:12]),
             reads=(s6b,), writes=(mvb,), drain_before=True, drain=True)
        S.op("act", lambda h: h.activation(out=nm[:, 0:1], in_=mvt[:, 0:1],
                                           func=AF.Copy, scale=-1.0),
             reads=(mvb,), writes=(nmb,))
        S.op("act", lambda h: h.activation(out=rs[:, 0:1], in_=mvt[:, 1:2],
                                           func=AF.Ln, bias=epsc[:, 0:1]),
             reads=(mvb, epsc_b), writes=(rsb,))
        S.op("act", lambda h: h.activation(out=rs[:, 0:1], in_=rs[:, 0:1],
                                           func=AF.Exp, scale=-0.5),
             reads=(rsb,), writes=(rsb,))
        S.op("dve", lambda h: h.tensor_scalar(xnt[:, :], xt_ap[:, :], nm[:, 0:1],
                                              rs[:, 0:1], op0=ALU.add, op1=ALU.mult),
             reads=(xt_b, nmb, rsb), writes=(xnb,))
        return xnt, xnb

    def transpose_bf(src_ap, src_b, bank, dsth, dsthb, dstl, dstlb, dcol):
        pt, ptb = pb[bank], pbuf[bank]
        S.op("pe", lambda h: h.transpose(pt[:, 0:128], src_ap, ident[:, :]),
             reads=(src_b, ident_b), writes=(ptb,))
        S.op("act", lambda h: h.activation(out=dsth[:, dcol:dcol + 128],
                                           in_=pt[:, 0:128], func=AF.Copy),
             reads=(ptb,), writes=(dsthb,))
        S.op("dve", lambda h: h.tensor_tensor(dstl[:, dcol:dcol + 128], pt[:, 0:128],
                                              dsth[:, dcol:dcol + 128],
                                              op=ALU.subtract),
             reads=(ptb, dsthb), writes=(dstlb,))

    def transpose_f32(src_ap, src_b, bank, dst_ap, dst_b):
        pt, ptb = pb[bank], pbuf[bank]
        S.op("pe", lambda h: h.transpose(pt[:, 0:128], src_ap, ident[:, :]),
             reads=(src_b, ident_b), writes=(ptb,))
        S.op("dve", lambda h: h.tensor_copy(out=dst_ap, in_=pt[:, 0:128]),
             reads=(ptb,), writes=(dst_b,))

    def mm3(bank_ap, bank_b, lh_ap, ll_ap, rh_ap, rl_ap, first, last, rbufs):
        S.op("pe", lambda h: h.matmul(bank_ap, lh_ap, rh_ap, start=first,
                                      stop=False),
             reads=rbufs, writes=(bank_b,))
        S.op("pe", lambda h: h.matmul(bank_ap, lh_ap, rl_ap, start=False,
                                      stop=False),
             reads=rbufs, writes=(bank_b,))
        S.op("pe", lambda h: h.matmul(bank_ap, ll_ap, rh_ap, start=False,
                                      stop=last),
             reads=rbufs, writes=(bank_b,))

    def dump_f32(idx, src_ap, src_b, ncols):
        ot, ob = x2o[0]
        done = 0
        while done < ncols:
            n = min(1024, ncols - done)
            S.op("dve", lambda h, done=done, n=n: h.tensor_copy(
                out=ot[:, 0:n], in_=src_ap[:, done:done + n]),
                reads=(src_b,), writes=(ob,))
            dma("gp", d_dbg[idx][:, done:done + n], ot[:, 0:n], (ob,), (), "dg")
            done += n

    astack = ExitStack()
    sba = mksb(astack)
    maskA, maskA_b = sba("maskA_s", [128, 2048], BF16)
    maskB, maskB_b = sba("maskB_s", [128, 1024], BF16)
    sel8, sel8_b = sba("sel8_s", [8, 512], F32)
    onesel, onesel_b = sba("onesel_s", [128, 64], BF16)
    wstage = [xt[0]]
    wA = [sba(f"wA{hl}", [128, 4096], BF16) for hl in range(2)]
    wB = [sba(f"wB{hl}", [128, 4096], BF16) for hl in range(2)]
    wpT32, wpT32_b = sba("wpT32", [128, 2048], F32)
    xnTh, xnTh_b = sba("xnTh", [128, 4096], BF16)
    xnTl, xnTl_b = sba("xnTl", [128, 4096], BF16)
    qT = [[sba(f"qT{hl}_{p}", [128, OWN], BF16) for p in range(4)]
          for hl in range(2)]
    kT = [[[sba(f"kT{hl}_{p}_{jp}", [128, 512], BF16) for jp in range(2)]
           for p in range(4)] for hl in range(2)]
    vv = [[sba(f"v{hl}_{jp}", [128, 2048], BF16) for jp in range(2)]
          for hl in range(2)]
    p32 = [sba(f"p32_{i}", [128, 512], F32) for i in range(4)]
    ph = [sba(f"ph{i}", [128, 512], BF16) for i in range(4)]
    pl = [sba(f"pl{i}", [128, 512], BF16) for i in range(4)]
    acc = [sba(f"acc{p}", [128, OWN], F32) for p in range(4)]
    sums, sums_b = sba("sums", [8, OWN], F32)

    for half in range(2):
        _xt, _xtb = xt[0]
        dma("sp", _xt[:, :], d_maskA[:, 1024 * half:1024 * half + 1024],
            (), (_xtb,), "dx")
        S.op("dve", lambda h, _xt=_xt, half=half: h.tensor_copy(
            out=maskA[:, 1024 * half:1024 * half + 1024], in_=_xt[:, :]),
            reads=(_xtb,), writes=(maskA_b,))
    _xt, _xtb = xt[0]
    dma("sp", _xt[:, :], d_maskB[:, :], (), (_xtb,), "dx")
    S.op("dve", lambda h, _xt=_xt: h.tensor_copy(out=maskB[:, :], in_=_xt[:, :]),
         reads=(_xtb,), writes=(maskB_b,))
    dma("sp", sel8[0:8, :], d_sel8[:, :], (), (sel8_b,), "dx")
    _xt, _xtb = xt[0]
    dma("sp", _xt[:, 0:64], d_onesel[:, :], (), (_xtb,), "dx")
    S.op("dve", lambda h, _xt=_xt: h.tensor_copy(out=onesel[:, :],
                                                 in_=_xt[:, 0:64]),
         reads=(_xtb,), writes=(onesel_b,))

    def load_wT_split(dW, rows0, wdst):
        for r in range(4):
            ws, wsb = wstage[0]
            dma("sp", ws[:, :], dW[rows0 + 128 * r:rows0 + 128 * r + 128, :],
                (), (wsb,), "dx")
            for cc in range(NCH):
                bank = 4 + (cc % 2)
                transpose_bf(ws[:, 128 * cc:128 * cc + 128], wsb, bank,
                             wdst[0][0], wdst[0][1], wdst[1][0], wdst[1][1],
                             512 * cc + 128 * r)

    def xnT_tile(xnt, xnb, t):
        for ccg in range(2):
            bank = (2 * (t % 2) + ccg) % 4
            pt, ptb = pb[bank], pbuf[bank]
            for q in range(4):
                cc = 4 * ccg + q
                S.op("pe", lambda h, pt=pt, q=q, cc=cc, xnt=xnt: h.transpose(
                    pt[:, 128 * q:128 * q + 128],
                    xnt[:, 128 * cc:128 * cc + 128], ident[:, :]),
                    reads=(xnb, ident_b), writes=(ptb,))
            base = 512 * 4 * ccg + 128 * t
            oap_h = bass.AP(xnTh, base, [[4096, 128], [512, 4], [1, 128]])
            oap_l = bass.AP(xnTl, base, [[4096, 128], [512, 4], [1, 128]])
            iap = bass.AP(pb[bank], 0, [[512, 128], [128, 4], [1, 128]])
            S.op("act", lambda h, oap_h=oap_h, iap=iap: h.activation(
                out=oap_h, in_=iap, func=AF.Copy),
                reads=(ptb,), writes=(xnTh_b,))
            S.op("dve", lambda h, oap_l=oap_l, iap=iap, oap_h=oap_h:
                 h.tensor_tensor(oap_l, iap, oap_h, op=ALU.subtract),
                 reads=(ptb, xnTh_b), writes=(xnTl_b,))

    def gen_xnT(x_dram, row0, which_bufs=None):
        lnres = []
        for t in range(4):
            i = t % 2
            xtt, xtb = xt[i]
            dma("sp", xtt[:, :], x_dram[row0 + 128 * t:row0 + 128 * t + 128, :],
                (), (xtb,), "dx")
            lnres.append(ln_tile(i, xtt, xtb))
            if t >= 1:
                xnt, xnb = lnres[t - 1]
                xnT_tile(xnt, xnb, t - 1)
        xnt, xnb = lnres[3]
        xnT_tile(xnt, xnb, 3)

    def proj_pair(wT, dsth, dsthb, dstl, dstlb, dcol, pair, n, bank):
        bk, bkb = pb[bank], pbuf[bank]
        for cc in range(NCH):
            lh = wT[0][0][:, 512 * cc + 128 * pair:512 * cc + 128 * pair + 128]
            ll = wT[1][0][:, 512 * cc + 128 * pair:512 * cc + 128 * pair + 128]
            mm3(bk[:, 0:n], bkb, lh, ll, xnTh[:, 512 * cc:512 * cc + n],
                xnTl[:, 512 * cc:512 * cc + n],
                cc == 0, cc == NCH - 1, (wT[0][1], wT[1][1], xnTh_b, xnTl_b))
        S.op("act", lambda h: h.activation(out=dsth[:, dcol:dcol + n],
                                           in_=bk[:, 0:n], func=AF.Copy),
             reads=(bkb,), writes=(dsthb,))
        S.op("dve", lambda h: h.tensor_tensor(dstl[:, dcol:dcol + n], bk[:, 0:n],
                                              dsth[:, dcol:dcol + n],
                                              op=ALU.subtract),
             reads=(bkb, dsthb), writes=(dstlb,))

    ngroups = 2
    for g2 in range(ngroups):
        load_wT_split(d_wq, 512 * g2, wA)
        for s in range(4):
            gen_xnT(d_xq, 512 * s)
            if stage == "xnq" and g2 == 0 and s == 0:
                for cc in range(2):
                    dump_f32(2 * cc, xnT[0][cc][0][:, :], xnT[0][cc][1], 512)
                    dump_f32(2 * cc + 1, xnT[1][cc][0][:, :], xnT[1][cc][1], 512)
                break
            for p in range(4):
                proj_pair(wA, qT[0][p][0], qT[0][p][1], qT[1][p][0], qT[1][p][1],
                          512 * s, p, 512, 6 + (p % 2))
        if stage == "xnq":
            break
        if stage == "qkv" and g2 == 0:
            dump_f32(0, qT[0][0][0][:, :], qT[0][0][1], 2048)
            dump_f32(1, qT[1][0][0][:, :], qT[1][0][1], 2048)
            dump_f32(2, qT[0][3][0][:, :], qT[0][3][1], 2048)
            dump_f32(3, qT[1][3][0][:, :], qT[1][3][1], 2048)
            break

        load_wT_split(d_wk, 512 * g2, wA)
        load_wT_split(d_wv, 512 * g2, wB)

        def kv_chunks(jn):
            par = jn % 2
            chunks = []

            def mk_xtile(t):
                def em():
                    i = t % 2
                    xtt, xtb = xt[i]
                    dma("sp", xtt[:, :],
                        d_xkv[512 * jn + 128 * t:512 * jn + 128 * t + 128, :],
                        (), (xtb,), "dx")
                    xnt, xnb = ln_tile(i, xtt, xtb)
                    xnT_tile(xnt, xnb, t)
                return em

            def mk_kpair(p):
                def em():
                    proj_pair(wA, kT[0][p][par][0], kT[0][p][par][1],
                              kT[1][p][par][0], kT[1][p][par][1],
                              0, p, 512, 6)
                return em

            def mk_vchunk(m):
                def em():
                    bk, bkb = pb[6], pbuf[6]
                    for cc in range(NCH):
                        mm3(bk[:, :], bkb,
                            xnTh[:, 512 * cc + 128 * m:
                                 512 * cc + 128 * m + 128],
                            xnTl[:, 512 * cc + 128 * m:
                                 512 * cc + 128 * m + 128],
                            wB[0][0][:, 512 * cc:512 * cc + 512],
                            wB[1][0][:, 512 * cc:512 * cc + 512],
                            cc == 0, cc == NCH - 1,
                            (xnTh_b, xnTl_b, wB[0][1], wB[1][1]))
                    vh_t, vh_b = vv[0][par]
                    vl_t, vl_b = vv[1][par]
                    S.op("act", lambda h, bk=bk, vh_t=vh_t: h.activation(
                        out=vh_t[:, 512 * m:512 * m + 512], in_=bk[:, :],
                        func=AF.Copy),
                        reads=(bkb,), writes=(vh_b,))
                    S.op("dve", lambda h, bk=bk, vh_t=vh_t, vl_t=vl_t:
                         h.tensor_tensor(
                             vl_t[:, 512 * m:512 * m + 512], bk[:, :],
                             vh_t[:, 512 * m:512 * m + 512], op=ALU.subtract),
                         reads=(bkb, vh_b), writes=(vl_b,))
                return em

            for t in range(4):
                chunks.append(mk_xtile(t))
            for p in range(4):
                chunks.append(mk_kpair(p))
            for m in range(4):
                chunks.append(mk_vchunk(m))
            return chunks

        for ch in kv_chunks(0):
            ch()
        for j in range(NKV):
            atoms = []
            for c in range(4):
                if j > 2 * c + 1:
                    continue
                if j == 2 * c + 1:
                    qt0, qn, msk = 512 * c + 256, 256, "B"
                elif j == 2 * c:
                    qt0, qn, msk = 512 * c, 512, "A"
                else:
                    qt0, qn, msk = 512 * c, 512, None
                smb, sm_bb = pb[7], pbuf[7]
                def mk_pre(p, m, qt0=None, qn=None, msk=None, j=None):
                    def emit_pre():
                        be, beb = pb[2 * (m % 2)], pbuf[2 * (m % 2)]
                        bo_, bob = pb[2 * (m % 2) + 1], pbuf[2 * (m % 2) + 1]
                        for (kw, qw, frst, lst) in ((0, 0, True, False),
                                                    (0, 1, False, False),
                                                    (1, 0, False, True)):
                            kh, khb = kT[kw][p][j % 2]
                            qh, qhb = qT[qw][p]
                            S.op("pe", lambda h, kh=kh, qh=qh, be=be,
                                 frst=frst, lst=lst: h.matmul(
                                     be[:, 0:qn],
                                     kh[0:64, 128 * m:128 * m + 128],
                                     qh[0:64, qt0:qt0 + qn],
                                     start=frst, stop=lst,
                                     tile_position=(0, 0)),
                                 reads=(khb, qhb), writes=(beb,))
                            S.op("pe", lambda h, kh=kh, qh=qh, bo_=bo_,
                                 frst=frst, lst=lst: h.matmul(
                                     bo_[:, 0:qn],
                                     kh[64:128, 128 * m:128 * m + 128],
                                     qh[64:128, qt0:qt0 + qn],
                                     start=frst, stop=lst,
                                     tile_position=(64, 0)),
                                 reads=(khb, qhb), writes=(bob,))
                        if msk == "A":
                            for bx, bxb in ((be, beb), (bo_, bob)):
                                S.op("dve", lambda h, bx=bx: h.tensor_tensor(
                                    bx[:, 0:512], bx[:, 0:512],
                                    maskA[:, 512 * m:512 * m + 512],
                                    op=ALU.add),
                                    reads=(bxb, maskA_b), writes=(bxb,))
                        elif msk == "B":
                            for bx, bxb in ((be, beb), (bo_, bob)):
                                S.op("dve", lambda h, bx=bx: h.tensor_tensor(
                                    bx[:, 0:256], bx[:, 0:256],
                                    maskB[:, 256 * m:256 * m + 256],
                                    op=ALU.add),
                                    reads=(bxb, maskB_b), writes=(bxb,))
                        for hh, (bx, bxb) in enumerate(((be, beb), (bo_, bob))):
                            pidx = 2 * (m % 2) + hh
                            p32t, p32b = p32[pidx]
                            pht, phb = ph[pidx]
                            plt, plb = pl[pidx]
                            S.op("act", lambda h, bx=bx, p32t=p32t:
                                 h.activation(out=p32t[:, 0:qn],
                                              in_=bx[:, 0:qn],
                                              func=AF.Exp, scale=QK_SCALE),
                                 reads=(bxb,), writes=(p32b,))
                            S.op("dve", lambda h, p32t=p32t, pht=pht:
                                 h.tensor_copy(out=pht[:, 0:qn],
                                               in_=p32t[:, 0:qn]),
                                 reads=(p32b,), writes=(phb,))
                            S.op("dve", lambda h, p32t=p32t, pht=pht, plt=plt:
                                 h.tensor_tensor(
                                     plt[:, 0:qn], p32t[:, 0:qn], pht[:, 0:qn],
                                     op=ALU.subtract),
                                 reads=(p32b, phb), writes=(plb,))
                    return emit_pre

                def mk_post(p, m, qt0=None, qn=None, j=None):
                    def emit_post():
                        for ti, (vw, pw) in enumerate(((0, 0), (0, 1), (1, 0))):
                            vt, vtb = vv[vw][j % 2]
                            for hh in range(2):
                                pt, ptb = (ph if pw == 0
                                           else pl)[2 * (m % 2) + hh]
                                vcol = 512 * m + 128 * p + 64 * hh
                                S.op("pe", lambda h, vt=vt, pt=pt, vcol=vcol,
                                     hh=hh, bk=av_bk[hh],
                                     st=(m == 0 and ti == 0),
                                     sp=(m == 3 and ti == 2): h.matmul(
                                         bk[64 * hh:64 * hh + 64, 0:qn],
                                         vt[:, vcol:vcol + 64], pt[:, 0:qn],
                                         start=st, stop=sp,
                                         tile_position=(0, 64 * hh)),
                                     reads=(vtb, ptb), writes=(av_bb2[hh],))
                        for hh in range(2):
                            q8 = 2 * p + hh
                            for pw in range(2):
                                pt, ptb = (ph if pw == 0
                                           else pl)[2 * (m % 2) + hh]
                                S.op("pe", lambda h, q8=q8, pt=pt,
                                     st=(p == 0 and m == 0 and hh == 0
                                         and pw == 0),
                                     sp=(p == 3 and m == 3 and hh == 1
                                         and pw == 1):
                                     h.matmul(smb[0:8, 0:qn],
                                              onesel[:, 8 * q8:8 * q8 + 8],
                                              pt[:, 0:qn], start=st, stop=sp),
                                     reads=(onesel_b, ptb), writes=(sm_bb,))
                        if m == 3:
                            a_t, a_b = acc[p]
                            for hh in range(2):
                                r0, r1 = 64 * hh, 64 * hh + 64
                                bk = av_bk[hh]
                                if j == 0:
                                    S.op("dve", lambda h, a_t=a_t, bk=bk,
                                         r0=r0, r1=r1:
                                         h.tensor_copy(
                                             out=a_t[r0:r1, qt0:qt0 + qn],
                                             in_=bk[r0:r1, 0:qn]),
                                         reads=(av_bb2[hh],), writes=(a_b,))
                                else:
                                    S.op("dve", lambda h, a_t=a_t, bk=bk,
                                         r0=r0, r1=r1:
                                         h.tensor_tensor(
                                             a_t[r0:r1, qt0:qt0 + qn],
                                             bk[r0:r1, 0:qn],
                                             a_t[r0:r1, qt0:qt0 + qn],
                                             op=ALU.add),
                                         reads=(av_bb2[hh], a_b),
                                         writes=(a_b,))
                    return emit_post

                av_bk = (pb[4], pb[5])
                av_bb2 = (pbuf[4], pbuf[5])

                def mk_sums_evac(qt0, qn, j):
                    def em():
                        if j == 0:
                            S.op("dve", lambda h: h.tensor_copy(
                                out=sums[0:8, qt0:qt0 + qn],
                                in_=smb[0:8, 0:qn]),
                                reads=(sm_bb,), writes=(sums_b,))
                        else:
                            S.op("dve", lambda h: h.tensor_tensor(
                                sums[0:8, qt0:qt0 + qn], smb[0:8, 0:qn],
                                sums[0:8, qt0:qt0 + qn], op=ALU.add),
                                reads=(sm_bb, sums_b), writes=(sums_b,))
                    return em

                for p in range(4):
                    for m in range(4):
                        atoms.append((mk_pre(p, m, qt0, qn, msk, j),
                                      mk_post(p, m, qt0, qn, j),
                                      mk_sums_evac(qt0, qn, j)
                                      if (p == 3 and m == 3) else None))

            nxt = kv_chunks(j + 1) if j + 1 < NKV else []
            n = len(atoms)
            done_ch = 0
            atoms[0][0]()
            for k in range(1, n):
                atoms[k][0]()
                atoms[k - 1][1]()
                if atoms[k - 1][2]:
                    atoms[k - 1][2]()
                want = (k * len(nxt)) // n
                while done_ch < want:
                    nxt[done_ch]()
                    done_ch += 1
            atoms[-1][1]()
            if atoms[-1][2]:
                atoms[-1][2]()
            while done_ch < len(nxt):
                nxt[done_ch]()
                done_ch += 1

        S.op("dve", lambda h: h.reciprocal(sums[0:8, :], sums[0:8, :]),
             reads=(sums_b,), writes=(sums_b,), drain_before=True, drain=True)
        for p in range(4):
            a_t, a_b = acc[p]
            for ch in range(4):
                bk, bkb = pb[6], pbuf[6]
                S.op("pe", lambda h, p=p, ch=ch, bk=bk: h.matmul(
                    bk[:, 0:512], sel8[0:8, 128 * p:128 * p + 128],
                    sums[0:8, 512 * ch:512 * ch + 512],
                    start=True, stop=True),
                    reads=(sel8_b, sums_b), writes=(bkb,))
                S.op("dve", lambda h, a_t=a_t, ch=ch, bk=bk: h.tensor_tensor(
                    a_t[:, 512 * ch:512 * ch + 512],
                    a_t[:, 512 * ch:512 * ch + 512], bk[:, 0:512], op=ALU.mult),
                    reads=(a_b, bkb), writes=(a_b,))
        if stage == "attn":
            for p in range(4):
                dump_f32(p, acc[p][0][:, :], acc[p][1], 2048)
            break

        for pn in range(2):
            for r in range(4):
                for bc in range(4):
                    bcol = 4 * pn + bc
                    ws, wsb = wstage[0]
                    dma("sp", ws[:, 0:128],
                        d_wp[128 * bcol:128 * bcol + 128,
                             512 * g2 + 128 * r:512 * g2 + 128 * r + 128],
                        (), (wsb,), "dx")
                    transpose_f32(ws[:, 0:128], wsb, 4 + ((r + bc) % 2),
                                  wpT32[:, 512 * r + 128 * bc:
                                        512 * r + 128 * bc + 128], wpT32_b)
            for tt in range(NT):
                i = tt % 2
                xtt, xtb = xt[i]
                if g2 == 0:
                    dma("sp", xtt[:, 512 * pn:512 * pn + 512],
                        d_xq[128 * tt:128 * tt + 128,
                             512 * pn:512 * pn + 512],
                        (), (xtb,), "dx")
                else:
                    dma("sp", xtt[:, 512 * pn:512 * pn + 512],
                        d_x2[128 * tt:128 * tt + 128,
                             512 * pn:512 * pn + 512],
                        (x2_half[tt][pn],), (xtb,), "dx")
                ot, ob = x2o[0]
                bk, bkb = pb[6 + (tt % 2)], pbuf[6 + (tt % 2)]
                for p in range(4):
                    a_t, a_b = acc[p]
                    S.op("pe", lambda h, a_t=a_t, p=p, tt=tt, bk=bk:
                         h.matmul(bk[:, 0:512],
                                  a_t[:, 128 * tt:128 * tt + 128],
                                  wpT32[:, 512 * p:512 * p + 512],
                                  start=(p == 0), stop=(p == 3)),
                         reads=(a_b, wpT32_b), writes=(bkb,))
                S.op("dve", lambda h, ot=ot, pn=pn, bk=bk, xtt=xtt:
                     h.tensor_tensor(ot[:, 512 * pn:512 * pn + 512],
                                     bk[:, 0:512],
                                     xtt[:, 512 * pn:512 * pn + 512],
                                     op=ALU.add),
                     reads=(bkb, xtb), writes=(ob,))
                dma("gp", d_x2[128 * tt:128 * tt + 128, 512 * pn:512 * pn + 512],
                    ot[:, 512 * pn:512 * pn + 512], (ob,),
                    (x2_half[tt][pn],), "dg")

    astack.close()

    if stage in ("full", "ffn1"):
        S.barrier()
        fstack = ExitStack()
        sbf = mksb(fstack)
        wm32, wm32_b = sbf("wm32", [128, 8192], F32)
        wos = [sbf("wos0", [128, 1024], F32)]
        emb32, emb32_b = sbf("emb32", [128, NEXP], F32)
        sel16, sel16_b = sbf("sel16_s", [16, 2048], BF16)
        b2T, b2T_b = sbf("b2T", [16, 128], F32)
        xn2T = [sbf(f"xn2T{cc}", [128, 512], F32) for cc in range(NCH)]
        hT32, hT32_b = sbf("hT32", [128, 4096], F32)
        tTh, tTh_b = sbf("tTh", [128, 4096], BF16)
        tTl, tTl_b = sbf("tTl", [128, 4096], BF16)
        lg, lg_b = sbf("lg", [128, 128], F32)
        ew4, ew4_b = sbf("ew4", [128, 64], F32)
        aux4, aux4_b = sbf("aux4", [128, 64], F32)
        nm4, nm4_b = sbf("nm4", [128, 4], F32)
        sc4, sc4_b = sbf("sc4", [128, 4], F32)
        m14, m14_b = sbf("m14", [128, 4], F32)
        cmat4 = [sbf(f"cmat{i}", [128, 128], F32) for i in range(2)]
        cTh = [sbf(f"cTh{i}", [16, 4096], BF16) for i in range(2)]
        cTl = [sbf(f"cTl{i}", [16, 4096], BF16) for i in range(2)]
        b2h, b2h_b = sbf("b2h", [16, 128], BF16)
        b2l, b2l_b = sbf("b2l", [16, 128], BF16)
        w1s = [sbf(f"w1s{i}", [128, FI], F32) for i in range(2)]
        w1h = [sbf(f"w1h{i}", [128, FI], BF16) for i in range(2)]
        w1l = [sbf(f"w1l{i}", [128, FI], BF16) for i in range(2)]
        w2s = [sbf(f"w2s{i}", [128, FI], F32) for i in range(2)]
        b1s = [sbf(f"b1s{i}", [128, 4], F32) for i in range(2)]
        bms, bms_b = sbf("bms", [128, 8], F32)
        bos, bos_b = sbf("bos", [1, D], F32)
        bobc, bobc_b = sbf("bobc", [128, D], F32)
        onesr, onesr_b = sbf("onesr", [1, 128], F32)
        hmid = [sbf(f"hmid{i}", [128, 2048], F32) for i in range(2)]
        comb, comb_b = sbf("comb", [128, 4096], F32)
        cbc, cbc_b = sbf("cbc", [128, 512], F32)
        tmp1, tmp1_b = sbf("tmp1", [128, 512], F32)

        dma("sp", bms[:, :], d_bm[:, :], (), (bms_b,), "dx")
        dma("sp", bos[0:1, :], d_bo[0:1, :], (), (bos_b,), "dx")
        dma("sp", emb32[:, :], d_emb[:, :], (), (emb32_b,), "dx")
        _st, _stb = hmid[0]
        dma("sp", _st[0:16, :], d_sel16[:, :], (), (_stb,), "dx")
        S.op("dve", lambda h, _st=_st: h.tensor_copy(out=sel16[0:16, :],
                                                     in_=_st[0:16, :]),
             reads=(_stb,), writes=(sel16_b,))
        dma("sp", b2T[0:16, :], d_b2[:, :], (), (b2T_b,), "dx")
        S.op("dve", lambda h: h.tensor_copy(out=b2h[0:16, :], in_=b2T[0:16, :]),
             reads=(b2T_b,), writes=(b2h_b,))
        S.op("dve", lambda h: h.tensor_tensor(b2l[0:16, :], b2T[0:16, :],
                                              b2h[0:16, :], op=ALU.subtract),
             reads=(b2T_b, b2h_b), writes=(b2l_b,))
        for r in range(NCH):
            dma("sp", wm32[:, 1024 * r:1024 * r + 1024],
                d_wm[128 * r:128 * r + 128, :], (), (wm32_b,), "dx")
        S.op("dve", lambda h: h.memset(onesr[0:1, :], 1.0), reads=(),
             writes=(onesr_b,))
        for chn in range(2):
            bk, bkb = pb[6], pbuf[6]
            S.op("pe", lambda h, chn=chn, bk=bk: h.matmul(
                bk[:, 0:512], onesr[0:1, :], bos[0:1, 512 * chn:512 * chn + 512],
                start=True, stop=True),
                reads=(onesr_b, bos_b), writes=(bkb,))
            S.op("dve", lambda h, chn=chn, bk=bk: h.tensor_copy(
                out=bobc[:, 512 * chn:512 * chn + 512], in_=bk[:, 0:512]),
                reads=(bkb,), writes=(bobc_b,))

        def prologue_chunks(sx):
            par = sx % 2
            cth_t, cth_b = cTh[par]
            ctl_t, ctl_b = cTl[par]
            chunks = []

            def mk_tile(t):
                def em():
                    x2tt, x2tb = xt[t % 2]
                    dma("sp", x2tt[:, :],
                        d_x2[512 * sx + 128 * t:512 * sx + 128 * t + 128, :],
                        (x2_half[4 * sx + t][0], x2_half[4 * sx + t][1]),
                        (x2tb,), "dx")
                    xnt, xnb = ln_tile(t % 2, x2tt, x2tb)
                    for cc in range(NCH):
                        dt_t, dt_b = xn2T[cc]
                        transpose_f32(xnt[:, 128 * cc:128 * cc + 128], xnb,
                                      cc % 4, dt_t[:, 128 * t:128 * t + 128],
                                      dt_b)
                return em

            def mk_wm(h0):
                def em():
                    for hc in (h0, h0 + 1):
                        bk, bkb = pb[6 + (hc % 2)], pbuf[6 + (hc % 2)]
                        for cc in range(NCH):
                            xt_t, xt_b2 = xn2T[cc]
                            S.op("pe", lambda h, hc=hc, cc=cc, bk=bk,
                                 xt_t=xt_t: h.matmul(
                                     bk[:, 0:512],
                                     wm32[:, 1024 * cc + 128 * hc:
                                          1024 * cc + 128 * hc + 128],
                                     xt_t[:, 0:512], start=(cc == 0),
                                     stop=(cc == NCH - 1)),
                                 reads=(wm32_b, xt_b2), writes=(bkb,))
                        S.op("dve", lambda h, hc=hc, bk=bk: h.tensor_scalar(
                            hT32[:, 512 * hc:512 * hc + 512], bk[:, 0:512],
                            bms[:, hc:hc + 1], None, op0=ALU.add),
                            reads=(bkb, bms_b), writes=(hT32_b,))
                return em

            def mk_router(tm):
                cmat, cmat_b = cmat4[tm % 2]
                def em():
                    bk, bkb = pb[tm % 2], pbuf[tm % 2]
                    for jj in range(NJ):
                        S.op("pe", lambda h, jj=jj, bk=bk: h.matmul(
                            bk[:, 16 * jj:16 * jj + 16],
                            hT32[:, 512 * jj + 128 * tm:
                                 512 * jj + 128 * tm + 128],
                            emb32[:, :], start=True, stop=True),
                            reads=(hT32_b, emb32_b), writes=(bkb,))
                    S.op("dve", lambda h, bk=bk: h.tensor_copy(
                        out=lg[:, :], in_=bk[:, 0:128]),
                        reads=(bkb,), writes=(lg_b,))
                    for jb in range(0, NJ, 4):
                        def sl(q, t16):
                            return t16[:, 16 * q:16 * q + 16]
                        for q in range(4):
                            S.op("dve", lambda h, q=q, jj=jb + q: h.reduce_max(
                                nm4[:, q:q + 1], lg[:, 16 * jj:16 * jj + 16],
                                axis=AX.X, negate=True),
                                reads=(lg_b,), writes=(nm4_b,),
                                drain_before=(q == 0), drain=(q == 3))
                        for q in range(4):
                            S.op("act", lambda h, q=q, jj=jb + q: h.activation(
                                out=sl(q, ew4), in_=lg[:, 16 * jj:16 * jj + 16],
                                func=AF.Exp, bias=nm4[:, q:q + 1]),
                                reads=(lg_b, nm4_b), writes=(ew4_b,))
                        for q in range(4):
                            S.op("dve", lambda h, q=q: h.reduce_sum(
                                sc4[:, q:q + 1], sl(q, ew4), axis=AX.X),
                                reads=(ew4_b,), writes=(sc4_b,),
                                drain_before=(q == 0), drain=(q == 3))
                        for q in range(4):
                            S.op("dve", lambda h, q=q: h.reciprocal(
                                sc4[:, q:q + 1], sc4[:, q:q + 1]),
                                reads=(sc4_b,), writes=(sc4_b,),
                                drain_before=(q == 0), drain=(q == 3))
                        for q in range(4):
                            S.op("dve", lambda h, q=q: h.reduce_max(
                                m14[:, q:q + 1], sl(q, ew4), axis=AX.X),
                                reads=(ew4_b,), writes=(m14_b,),
                                drain_before=(q == 0), drain=(q == 3))
                        for q in range(4):
                            S.op("dve", lambda h, q=q: h.tensor_scalar(
                                sl(q, aux4), sl(q, ew4), m14[:, q:q + 1], None,
                                op0=ALU.is_equal),
                                reads=(ew4_b, m14_b), writes=(aux4_b,),
                                drain_before=(q == 0))
                        for q in range(4):
                            S.op("dve", lambda h, q=q: h.scalar_tensor_tensor(
                                sl(q, aux4), sl(q, aux4), -2.0, sl(q, ew4),
                                op0=ALU.mult, op1=ALU.add),
                                reads=(aux4_b, ew4_b), writes=(aux4_b,),
                                drain_before=(q == 0))
                        for q in range(4):
                            S.op("dve", lambda h, q=q: h.reduce_max(
                                m14[:, q:q + 1], sl(q, aux4), axis=AX.X),
                                reads=(aux4_b,), writes=(m14_b,),
                                drain_before=(q == 0), drain=(q == 3))
                        for q in range(4):
                            S.op("dve", lambda h, q=q: h.tensor_scalar(
                                sl(q, aux4), sl(q, ew4), m14[:, q:q + 1], None,
                                op0=ALU.is_ge),
                                reads=(ew4_b, m14_b), writes=(aux4_b,),
                                drain_before=(q == 0))
                        for q in range(4):
                            S.op("dve", lambda h, q=q, jj=jb + q:
                                 h.scalar_tensor_tensor(
                                     cmat[:, 16 * jj:16 * jj + 16], sl(q, ew4),
                                     sc4[:, q:q + 1], sl(q, aux4),
                                     op0=ALU.mult, op1=ALU.mult),
                                 reads=(ew4_b, sc4_b, aux4_b), writes=(cmat_b,),
                                 drain_before=(q == 0))
                return em

            def mk_ctrans(tm, cm_t, cm_b):
                def em():
                    for jj in range(NJ):
                        bk2, bk2b = pb[2 + (jj % 2)], pbuf[2 + (jj % 2)]
                        S.op("pe", lambda h, jj=jj, bk2=bk2, cm_t=cm_t:
                             h.transpose(
                                 bk2[0:16, 0:128],
                                 cm_t[:, 16 * jj:16 * jj + 16], ident[:, :]),
                             reads=(cm_b, ident_b), writes=(bk2b,))
                        col = 512 * jj + 128 * tm
                        S.op("act", lambda h, bk2=bk2, col=col: h.activation(
                            out=cth_t[0:16, col:col + 128],
                            in_=bk2[0:16, 0:128], func=AF.Copy),
                            reads=(bk2b,), writes=(cth_b,))
                        S.op("dve", lambda h, bk2=bk2, col=col: h.tensor_tensor(
                            ctl_t[0:16, col:col + 128], bk2[0:16, 0:128],
                            cth_t[0:16, col:col + 128], op=ALU.subtract),
                            reads=(bk2b, cth_b), writes=(ctl_b,))
                return em

            for t in range(4):
                chunks.append(mk_tile(t))
            for h0 in (0, 2, 4, 6):
                chunks.append(mk_wm(h0))
            rpend = []
            for tm in range(4):
                chunks.append(mk_router(tm))
                rpend.append(mk_ctrans(tm, *cmat4[tm % 2]))
                if tm >= 1:
                    chunks.append(rpend[tm - 1])
            chunks.append(rpend[3])
            return chunks

        def tT_split(sx):
            for hc in range(NCH):
                S.op("act", lambda h, hc=hc: h.activation(
                    out=tTh[:, 512 * hc:512 * hc + 512],
                    in_=hT32[:, 512 * hc:512 * hc + 512], func=AF.Copy),
                    reads=(hT32_b,), writes=(tTh_b,))
                S.op("dve", lambda h, hc=hc: h.tensor_tensor(
                    tTl[:, 512 * hc:512 * hc + 512],
                    hT32[:, 512 * hc:512 * hc + 512],
                    tTh[:, 512 * hc:512 * hc + 512], op=ALU.subtract),
                    reads=(hT32_b, tTh_b), writes=(tTl_b,))

        for ch in prologue_chunks(0):
            ch()
        tT_split(0)
        for s in range(4):
            par = s % 2
            cth_t, cth_b = cTh[par]
            ctl_t, ctl_b = cTl[par]
            nxt = prologue_chunks(s + 1) if s < 3 else []
            for e in range(NEXP):
                w1st, w1sb = w1s[e % 2]
                w1ht, w1hb = w1h[e % 2]
                w1lt, w1lb = w1l[e % 2]
                w2t, w2b = w2s[e % 2]
                b1t, b1b = b1s[e % 2]
                dma("sp", w1st[:, :], d_w1[e, :, :], (), (w1sb,), "dx")
                S.op("dve", lambda h, w1st=w1st, w1ht=w1ht: h.tensor_copy(
                    out=w1ht[:, :], in_=w1st[:, :]),
                    reads=(w1sb,), writes=(w1hb,))
                S.op("dve", lambda h, w1st=w1st, w1ht=w1ht, w1lt=w1lt:
                     h.tensor_tensor(w1lt[:, :], w1st[:, :], w1ht[:, :],
                                     op=ALU.subtract),
                     reads=(w1sb, w1hb), writes=(w1lb,))
                dma("sp", w2t[:, :], d_w2r[e, :, :], (), (w2b,), "dx")
                dma("sp", b1t[:, :], d_b1[e, :, :], (), (b1b,), "dx")
                for jj in range(NJ):
                    hm_t, hm_b = hmid[jj % 2]
                    for ic in range(4):
                        bk, bkb = pb[ic % 2], pbuf[ic % 2]
                        mm3(bk[:, 0:512], bkb,
                            w1ht[:, 128 * ic:128 * ic + 128],
                            w1lt[:, 128 * ic:128 * ic + 128],
                            tTh[:, 512 * jj:512 * jj + 512],
                            tTl[:, 512 * jj:512 * jj + 512],
                            True, True, (w1hb, w1lb, tTh_b, tTl_b))
                        S.op("act", lambda h, ic=ic, bk=bk, hm_t=hm_t, b1t=b1t:
                             h.activation(out=hm_t[:, 512 * ic:512 * ic + 512],
                                          in_=bk[:, 0:512], func=AF.Gelu,
                                          bias=b1t[:, ic:ic + 1]),
                             reads=(bkb, b1b), writes=(hm_b,))
                    bke, bkeb = pb[2 + (jj % 2)], pbuf[2 + (jj % 2)]
                    for ic in range(4):
                        S.op("pe", lambda h, ic=ic, bke=bke, w2t=w2t, hm_t=hm_t:
                             h.matmul(bke[:, 0:512],
                                      w2t[:, 128 * ic:128 * ic + 128],
                                      hm_t[:, 512 * ic:512 * ic + 512],
                                      start=(ic == 0), stop=(ic == 3)),
                             reads=(w2b, hm_b), writes=(bkeb,))
                    bkc, bkcb = pb[4], pbuf[4]
                    S.op("pe", lambda h, e=e, jj=jj, bkc=bkc, cth_t=cth_t:
                         h.matmul(
                             bkc[:, 0:512], sel16[0:16, 128 * e:128 * e + 128],
                             cth_t[0:16, 512 * jj:512 * jj + 512], start=True,
                             stop=False),
                         reads=(sel16_b, cth_b), writes=(bkcb,))
                    S.op("pe", lambda h, e=e, jj=jj, bkc=bkc, ctl_t=ctl_t:
                         h.matmul(
                             bkc[:, 0:512], sel16[0:16, 128 * e:128 * e + 128],
                             ctl_t[0:16, 512 * jj:512 * jj + 512], start=False,
                             stop=True),
                         reads=(sel16_b, ctl_b), writes=(bkcb,))
                    S.op("act", lambda h, bkc=bkc: h.activation(
                        out=cbc[:, 0:512], in_=bkc[:, 0:512], func=AF.Copy),
                        reads=(bkcb,), writes=(cbc_b,))
                    if e == 0:
                        S.op("dve", lambda h, jj=jj, bke=bke: h.tensor_tensor(
                            comb[:, 512 * jj:512 * jj + 512], bke[:, 0:512],
                            cbc[:, 0:512], op=ALU.mult),
                            reads=(bkeb, cbc_b), writes=(comb_b,))
                    else:
                        S.op("dve", lambda h, bke=bke: h.tensor_tensor(
                            tmp1[:, 0:512], bke[:, 0:512], cbc[:, 0:512],
                            op=ALU.mult),
                            reads=(bkeb, cbc_b), writes=(tmp1_b,))
                        S.op("dve", lambda h, jj=jj: h.tensor_tensor(
                            comb[:, 512 * jj:512 * jj + 512], tmp1[:, 0:512],
                            comb[:, 512 * jj:512 * jj + 512], op=ALU.add),
                            reads=(tmp1_b, comb_b), writes=(comb_b,))
                if nxt and 1 <= e <= len(nxt):
                    nxt[e - 1]()
            for k in range(NEXP - 1, len(nxt)):
                nxt[k]()
            for jj in range(NJ):
                bk, bkb = pb[5], pbuf[5]
                mm3(bk[:, 0:512], bkb, b2h[0:16, :], b2l[0:16, :],
                    cth_t[0:16, 512 * jj:512 * jj + 512],
                    ctl_t[0:16, 512 * jj:512 * jj + 512],
                    True, True, (b2h_b, b2l_b, cth_b, ctl_b))
                S.op("dve", lambda h, jj=jj, bk=bk: h.tensor_tensor(
                    comb[:, 512 * jj:512 * jj + 512], bk[:, 0:512],
                    comb[:, 512 * jj:512 * jj + 512], op=ALU.add),
                    reads=(bkb, comb_b), writes=(comb_b,))
            for dc in range(NCH):
                wot, wob = wos[0]
                dma("sp", wot[:, :], d_wo[128 * dc:128 * dc + 128, :],
                    (), (wob,), "dx")
                for tm in range(4):
                    for pn in range(2):
                        bk, bkb = pb[2 * tm + pn], pbuf[2 * tm + pn]
                        S.op("pe", lambda h, dc=dc, tm=tm, pn=pn, bk=bk,
                             wot=wot: h.matmul(
                                 bk[:, 0:512],
                                 comb[:, 512 * dc + 128 * tm:
                                      512 * dc + 128 * tm + 128],
                                 wot[:, 512 * pn:512 * pn + 512],
                                 start=(dc == 0), stop=(dc == NCH - 1)),
                             reads=(comb_b, wob), writes=(bkb,))
            for tm in range(4):
                x2tt, x2tb = xt[tm % 2]
                dma("sp", x2tt[:, :],
                    d_x2[512 * s + 128 * tm:512 * s + 128 * tm + 128, :],
                    (x2_half[4 * s + tm][0], x2_half[4 * s + tm][1]),
                    (x2tb,), "dx")
                ot, ob = x2o[0]
                for pn in range(2):
                    bk, bkb = pb[2 * tm + pn], pbuf[2 * tm + pn]
                    S.op("dve", lambda h, pn=pn, bk=bk: h.tensor_tensor(
                        tmp1[:, 0:512], bk[:, 0:512],
                        bobc[:, 512 * pn:512 * pn + 512], op=ALU.add),
                        reads=(bkb, bobc_b), writes=(tmp1_b,))
                    S.op("dve", lambda h, ot=ot, pn=pn, x2tt=x2tt:
                         h.tensor_tensor(ot[:, 512 * pn:512 * pn + 512],
                                         tmp1[:, 0:512],
                                         x2tt[:, 512 * pn:512 * pn + 512],
                                         op=ALU.add),
                         reads=(tmp1_b, x2tb), writes=(ob,))
                dma("gp", d_out[512 * s + 128 * tm:512 * s + 128 * tm + 128, :],
                    ot[:, :], (ob,), (), "dg")
            if s < 3:
                tT_split(s + 1)
        fstack.close()

    stack0.close()
    S.final_wait("gp", "dg")

    sems = {}
    for e in ENGINES:
        sems[e] = nc.alloc_semaphore("sem_" + e)
    for k in S.dcnt:
        sems[k] = nc.alloc_semaphore("sem_" + k[2:])
    S.prepare()
    with nc.Block() as block:
        @block.tensor
        def _(t):
            S.emit_engine("pe", t, sems)

        @block.scalar
        def _(sc):
            S.emit_engine("act", sc, sems)

        @block.vector
        def _(v):
            S.emit_engine("dve", v, sems)

        @block.gpsimd
        def _(gp_):
            S.emit_engine("gp", gp_, sems)

        @block.sync
        def _(sy):
            S.emit_engine("sp", sy, sems)
    return nc


def _host_inputs(inputs, core):
    b, g = core // 2, core % 2
    x_b = np.ascontiguousarray(np.asarray(inputs["L0_attn_x"])[b])
    x_q = np.ascontiguousarray(x_b.reshape(32, 128, D)[g::2].reshape(OWN, D))
    maskA, maskB = _causal_masks(g)
    sel8 = np.zeros((8, 512), dtype=np.float32)
    for p in range(4):
        sel8[2 * p, 128 * p:128 * p + 64] = 1.0 / 1024.0
        sel8[2 * p + 1, 128 * p + 64:128 * p + 128] = 1.0 / 1024.0
    onesel = np.zeros((128, 64), dtype=np.float32)
    for q in range(8):
        onesel[:, 8 * q + q] = 1.0
    sel16 = np.zeros((16, 2048), dtype=np.float32)
    for e in range(16):
        sel16[e, 128 * e:128 * e + 128] = 1.0
    b1 = np.asarray(inputs["L0_ffn_b1"]).reshape(NEXP, 4, 128).transpose(0, 2, 1)
    bm = np.asarray(inputs["L0_ffn_bm"]).reshape(8, 128).T
    return {
        "x_q": x_q,
        "x_kv": x_b,
        "w_q": np.asarray(inputs["L0_attn_W_query"]),
        "w_k": np.asarray(inputs["L0_attn_W_key"]),
        "w_v": np.asarray(inputs["L0_attn_W_value"]),
        "w_p": np.asarray(inputs["L0_attn_W_proj"]),
        "w_m": np.asarray(inputs["L0_ffn_Wm"]),
        "w_o": np.asarray(inputs["L0_ffn_Wo"]),
        "w_1": np.asarray(inputs["L0_ffn_W1"]),
        "w_2": np.ascontiguousarray(np.asarray(inputs["L0_ffn_W2"]).reshape(NEXP, 4, 128, 128).transpose(0, 2, 1, 3).reshape(NEXP, 128, FI)),
        "b_1": np.ascontiguousarray(b1),
        "b_2": np.asarray(inputs["L0_ffn_b2"]),
        "b_m": np.ascontiguousarray(bm),
        "b_o": np.asarray(inputs["L0_ffn_bo"]).reshape(1, D),
        "emb": np.asarray(inputs["L0_ffn_expert_emb"]),
        "maskA": maskA,
        "maskB": maskB,
        "ident": np.eye(128, dtype=np.float32),
        "sel8": sel8,
        "onesel": onesel,
        "sel16": sel16,
    }


def kernel(**inputs):
    nc = build_nc("full")
    in_maps = [_host_inputs(inputs, core) for core in range(8)]
    res = run_bass_kernel_spmd(nc, in_maps, core_ids=list(range(8)))
    y = np.empty((B, T, D), dtype=np.float32)
    for core in range(8):
        b, g = core // 2, core % 2
        out = np.asarray(res.results[core]["out"])
        y[b].reshape(32, 128, D)[g::2] = out.reshape(NT, 128, D)
    return y


# version 28
# speedup vs baseline: 1.0531x; 1.0021x over previous
from contextlib import ExitStack

import numpy as np
import ml_dtypes

import concourse.bass as bass
import concourse.mybir as mybir
from concourse.bass_utils import run_bass_kernel_spmd

F32 = mybir.dt.float32
BF16 = mybir.dt.bfloat16
AF = mybir.ActivationFunctionType
ALU = mybir.AluOpType
AX = mybir.AxisListType

B, T, D = 4, 4096, 1024
OWN = 2048
NT = 16
NKV = 8
NCH = 8
NEXP, FI = 16, 512
NJ = 8
EPS = 1e-5
QK_SCALE = 1.0 / 8192.0
AV_SCALE = 1.0 / 1024.0

ENGINES = ("pe", "act", "dve", "gp", "sp")


class Buf:
    __slots__ = ("name", "writer", "readers")

    def __init__(self, name):
        self.name = name
        self.writer = None
        self.readers = {}


class Sched:

    DPOOL = {"dx": 48, "dg": 12}

    def __init__(self):
        self.ops = []
        self.ecnt = {e: 0 for e in ENGINES}
        self.dcnt = {}
        self.didx = {}

    def op(self, eng, emit, reads=(), writes=(), dma=None, drain=False,
           drain_before=False):
        waits = {}

        def need(tok):
            if tok is None:
                return
            s, v = tok
            if s == eng and eng == "pe":
                return
            if waits.get(s, 0) < v:
                waits[s] = v

        for b in reads:
            need(b.writer)
        for b in writes:
            need(b.writer)
            for t in b.readers.items():
                need(t)
        if dma is None:
            self.ecnt[eng] += 1
            tok = (eng, self.ecnt[eng])
        else:
            i = self.didx.get(dma, 0)
            self.didx[dma] = (i + 1) % self.DPOOL[dma]
            key = f"d:{dma}{i}"
            prev = self.dcnt.get(key, 0)
            if prev:
                if waits.get(key, 0) < prev:
                    waits[key] = prev
            self.dcnt[key] = prev + 16
            tok = (key, self.dcnt[key])
        for b in reads:
            if b.readers.get(tok[0], 0) < tok[1]:
                b.readers[tok[0]] = tok[1]
        for b in writes:
            b.writer = tok
            b.readers = {}
        self.ops.append((eng, emit, waits, tok, dma,
                         (drain_before, drain)))
        return tok

    def barrier(self):
        toks = [(e, self.ecnt[e]) for e in ENGINES if self.ecnt[e] > 0]
        toks += list(self.dcnt.items())
        for e in ENGINES:
            waits = {s: v for s, v in toks if s != e}
            self.ecnt[e] += 1
            self.ops.append((e, None, waits, (e, self.ecnt[e]), None,
                             (False, False)))

    def final_wait(self, eng, queue):
        waits = {k: v for k, v in self.dcnt.items()
                 if k.startswith("d:" + queue) and v}
        if waits:
            self.ecnt[eng] += 1
            self.ops.append((eng, None, waits, (eng, self.ecnt[eng]), None,
                             (False, False)))

    def prepare(self):
        needed = set()
        for o in self.ops:
            needed.update(o[2].items())
        incmap = {}
        for e in ENGINES:
            vals = sorted(v for (s, v) in needed if s == e)
            for newv, origv in enumerate(vals, 1):
                incmap[(e, origv)] = newv
        self._needed = needed
        self._incmap = incmap

    def emit_engine(self, e, h, sems):
        dw = {}
        for _e, emit, waits, tok, dma, drain in self.ops:
            if _e != e:
                continue
            for s in sorted(waits):
                v = waits[s]
                if not s.startswith("d:"):
                    v = self._incmap[(s, v)]
                if dw.get(s, 0) >= v:
                    continue
                dw[s] = v
                h.wait_ge(sems[s], v)
            if emit is None:
                continue
            if drain[0]:
                h.drain()
            inst = emit(h)
            if drain[1]:
                d = h.drain()
                if d is not None:
                    inst = d
            if dma is not None:
                inst.then_inc(sems[tok[0]], 16)
            elif (e, tok[1]) in self._needed:
                inst.then_inc(sems[e], 1)


def _causal_masks(g):
    p = np.arange(128)[:, None, None, None]
    m = np.arange(4)[None, :, None, None]
    i4 = np.arange(4)[None, None, :, None]
    ql = np.arange(128)[None, None, None, :]
    okA = (128 * m + p) <= (256 * i4 + 128 * g + ql)
    i2 = np.arange(2, 4)[None, None, :, None]
    okB = (512 + 128 * m + p) <= (256 * i2 + 128 * g + ql)
    maskA = np.where(okA, 0.0, -1e30).astype(np.float32).reshape(128, 2048)
    maskB = np.where(okB, 0.0, -1e30).astype(np.float32).reshape(128, 1024)
    return maskA, maskB


def build_nc(stage="full"):
    nc = bass.Bass("TRN2", target_bir_lowering=False)

    d_xq = nc.dram_tensor("x_q", [OWN, D], F32, kind="ExternalInput").ap()
    d_xkv = nc.dram_tensor("x_kv", [T, D], F32, kind="ExternalInput").ap()
    d_wq = nc.dram_tensor("w_q", [D, D], F32, kind="ExternalInput").ap()
    d_wk = nc.dram_tensor("w_k", [D, D], F32, kind="ExternalInput").ap()
    d_wv = nc.dram_tensor("w_v", [D, D], F32, kind="ExternalInput").ap()
    d_wp = nc.dram_tensor("w_p", [D, D], F32, kind="ExternalInput").ap()
    d_wm = nc.dram_tensor("w_m", [D, D], F32, kind="ExternalInput").ap()
    d_wo = nc.dram_tensor("w_o", [D, D], F32, kind="ExternalInput").ap()
    d_w1 = nc.dram_tensor("w_1", [NEXP, 128, FI], F32, kind="ExternalInput").ap()
    d_w2r = nc.dram_tensor("w_2", [NEXP, 128, FI], F32,
                           kind="ExternalInput").ap()
    d_b1 = nc.dram_tensor("b_1", [NEXP, 128, 4], F32, kind="ExternalInput").ap()
    d_b2 = nc.dram_tensor("b_2", [NEXP, 128], F32, kind="ExternalInput").ap()
    d_bm = nc.dram_tensor("b_m", [128, 8], F32, kind="ExternalInput").ap()
    d_bo = nc.dram_tensor("b_o", [1, D], F32, kind="ExternalInput").ap()
    d_emb = nc.dram_tensor("emb", [128, NEXP], F32, kind="ExternalInput").ap()
    d_maskA = nc.dram_tensor("maskA", [128, 2048], F32, kind="ExternalInput").ap()
    d_maskB = nc.dram_tensor("maskB", [128, 1024], F32, kind="ExternalInput").ap()
    d_ident = nc.dram_tensor("ident", [128, 128], F32, kind="ExternalInput").ap()
    d_sel8 = nc.dram_tensor("sel8", [8, 512], F32, kind="ExternalInput").ap()
    d_onesel = nc.dram_tensor("onesel", [128, 64], F32, kind="ExternalInput").ap()
    d_sel16 = nc.dram_tensor("sel16", [16, 2048], F32, kind="ExternalInput").ap()
    d_out = nc.dram_tensor("out", [OWN, D], F32, kind="ExternalOutput").ap()
    x2kind = {"kind": "ExternalOutput"} if stage == "oproj" else {}
    d_x2 = nc.dram_tensor("x2scratch", [OWN, D], F32, **x2kind).ap()
    x2_bufs = [Buf(f"x2d{t}") for t in range(NT)]
    x2_half = [[Buf(f"x2d{t}h{p}") for p in range(2)] for t in range(NT)]

    ndbg = 4 if stage in ("xnq", "qkv", "attn", "ffn1") else 0
    d_dbg = [nc.dram_tensor(f"dbg{i}", [128, 2048], F32, kind="ExternalOutput").ap()
             for i in range(ndbg)]

    S = Sched()

    pb = [nc.alloc_psum_tensor(f"pb{i}", [128, 512], F32) for i in range(8)]
    pbuf = [Buf(f"pb{i}") for i in range(8)]

    stack0 = ExitStack()

    def mksb(stk):
        def sba(name, shape, dt):
            t = stk.enter_context(nc.sbuf_tensor(name, shape, dt))
            return t, Buf(name)
        return sba

    sb0 = mksb(stack0)

    ident, ident_b = sb0("identity", [128, 128], F32)
    xt = [sb0(f"xt{i}", [128, D], F32) for i in range(2)]
    xn = [sb0(f"xn{i}", [128, D], F32) for i in range(2)]
    stat6 = [sb0(f"stat6_{i}", [128, 12], F32) for i in range(2)]
    mvs = [sb0(f"mv{i}", [128, 2], F32) for i in range(2)]
    negm = [sb0(f"negm{i}", [128, 1], F32) for i in range(2)]
    rstd = [sb0(f"rstd{i}", [128, 1], F32) for i in range(2)]
    x2o = [sb0("x2o0", [128, D], F32)]
    epsc, epsc_b = sb0("epsc", [128, 1], F32)

    def dma(engq, dst_ap, src_ap, reads, writes, q):
        S.op(engq, lambda h, d=dst_ap, s=src_ap: h.dma_start(out=d, in_=s),
             reads=reads, writes=writes, dma=q)

    dma("sp", ident[:, :], d_ident[:, :], (), (ident_b,), "dx")
    S.op("dve", lambda h: h.memset(epsc[:, 0:1], EPS), reads=(), writes=(epsc_b,))

    def ln_tile(i, xt_ap, xt_b):
        s6, s6b = stat6[i]
        mvt, mvb = mvs[i]
        nm, nmb = negm[i]
        rs, rsb = rstd[i]
        xnt, xnb = xn[i % 2]
        S.op("dve", lambda h: h.bn_stats(s6[:, 0:6], xt_ap[:, 0:512]),
             reads=(xt_b,), writes=(s6b,), drain_before=True)
        S.op("dve", lambda h: h.bn_stats(s6[:, 6:12], xt_ap[:, 512:1024]),
             reads=(xt_b,), writes=(s6b,), drain=True)
        S.op("dve", lambda h: h.bn_aggr(mvt[:, 0:2], s6[:, 0:12]),
             reads=(s6b,), writes=(mvb,), drain_before=True, drain=True)
        S.op("act", lambda h: h.activation(out=nm[:, 0:1], in_=mvt[:, 0:1],
                                           func=AF.Copy, scale=-1.0),
             reads=(mvb,), writes=(nmb,))
        S.op("act", lambda h: h.activation(out=rs[:, 0:1], in_=mvt[:, 1:2],
                                           func=AF.Ln, bias=epsc[:, 0:1]),
             reads=(mvb, epsc_b), writes=(rsb,))
        S.op("act", lambda h: h.activation(out=rs[:, 0:1], in_=rs[:, 0:1],
                                           func=AF.Exp, scale=-0.5),
             reads=(rsb,), writes=(rsb,))
        S.op("dve", lambda h: h.tensor_scalar(xnt[:, :], xt_ap[:, :], nm[:, 0:1],
                                              rs[:, 0:1], op0=ALU.add, op1=ALU.mult),
             reads=(xt_b, nmb, rsb), writes=(xnb,))
        return xnt, xnb

    def transpose_bf(src_ap, src_b, bank, dsth, dsthb, dstl, dstlb, dcol):
        pt, ptb = pb[bank], pbuf[bank]
        S.op("pe", lambda h: h.transpose(pt[:, 0:128], src_ap, ident[:, :]),
             reads=(src_b, ident_b), writes=(ptb,))
        S.op("act", lambda h: h.activation(out=dsth[:, dcol:dcol + 128],
                                           in_=pt[:, 0:128], func=AF.Copy),
             reads=(ptb,), writes=(dsthb,))
        S.op("dve", lambda h: h.tensor_tensor(dstl[:, dcol:dcol + 128], pt[:, 0:128],
                                              dsth[:, dcol:dcol + 128],
                                              op=ALU.subtract),
             reads=(ptb, dsthb), writes=(dstlb,))

    def transpose_f32(src_ap, src_b, bank, dst_ap, dst_b):
        pt, ptb = pb[bank], pbuf[bank]
        S.op("pe", lambda h: h.transpose(pt[:, 0:128], src_ap, ident[:, :]),
             reads=(src_b, ident_b), writes=(ptb,))
        S.op("dve", lambda h: h.tensor_copy(out=dst_ap, in_=pt[:, 0:128]),
             reads=(ptb,), writes=(dst_b,))

    def mm3(bank_ap, bank_b, lh_ap, ll_ap, rh_ap, rl_ap, first, last, rbufs):
        S.op("pe", lambda h: h.matmul(bank_ap, lh_ap, rh_ap, start=first,
                                      stop=False),
             reads=rbufs, writes=(bank_b,))
        S.op("pe", lambda h: h.matmul(bank_ap, lh_ap, rl_ap, start=False,
                                      stop=False),
             reads=rbufs, writes=(bank_b,))
        S.op("pe", lambda h: h.matmul(bank_ap, ll_ap, rh_ap, start=False,
                                      stop=last),
             reads=rbufs, writes=(bank_b,))

    def dump_f32(idx, src_ap, src_b, ncols):
        ot, ob = x2o[0]
        done = 0
        while done < ncols:
            n = min(1024, ncols - done)
            S.op("dve", lambda h, done=done, n=n: h.tensor_copy(
                out=ot[:, 0:n], in_=src_ap[:, done:done + n]),
                reads=(src_b,), writes=(ob,))
            dma("gp", d_dbg[idx][:, done:done + n], ot[:, 0:n], (ob,), (), "dg")
            done += n

    astack = ExitStack()
    sba = mksb(astack)
    maskA, maskA_b = sba("maskA_s", [128, 2048], BF16)
    maskB, maskB_b = sba("maskB_s", [128, 1024], BF16)
    sel8, sel8_b = sba("sel8_s", [8, 512], F32)
    onesel, onesel_b = sba("onesel_s", [128, 64], BF16)
    wstage = [xt[0]]
    wA = [sba(f"wA{hl}", [128, 4096], BF16) for hl in range(2)]
    wB = [sba(f"wB{hl}", [128, 4096], BF16) for hl in range(2)]
    wpT32, wpT32_b = sba("wpT32", [128, 2048], F32)
    xnTh, xnTh_b = sba("xnTh", [128, 4096], BF16)
    xnTl, xnTl_b = sba("xnTl", [128, 4096], BF16)
    qT = [[sba(f"qT{hl}_{p}", [128, OWN], BF16) for p in range(4)]
          for hl in range(2)]
    kT = [[[sba(f"kT{hl}_{p}_{jp}", [128, 512], BF16) for jp in range(2)]
           for p in range(4)] for hl in range(2)]
    vv = [[sba(f"v{hl}_{jp}", [128, 2048], BF16) for jp in range(2)]
          for hl in range(2)]
    p32 = [sba(f"p32_{i}", [128, 512], F32) for i in range(4)]
    ph = [sba(f"ph{i}", [128, 512], BF16) for i in range(4)]
    pl = [sba(f"pl{i}", [128, 512], BF16) for i in range(4)]
    acc = [sba(f"acc{p}", [128, OWN], F32) for p in range(4)]
    sums, sums_b = sba("sums", [8, OWN], F32)

    for half in range(2):
        _xt, _xtb = xt[0]
        dma("sp", _xt[:, :], d_maskA[:, 1024 * half:1024 * half + 1024],
            (), (_xtb,), "dx")
        S.op("dve", lambda h, _xt=_xt, half=half: h.tensor_copy(
            out=maskA[:, 1024 * half:1024 * half + 1024], in_=_xt[:, :]),
            reads=(_xtb,), writes=(maskA_b,))
    _xt, _xtb = xt[0]
    dma("sp", _xt[:, :], d_maskB[:, :], (), (_xtb,), "dx")
    S.op("dve", lambda h, _xt=_xt: h.tensor_copy(out=maskB[:, :], in_=_xt[:, :]),
         reads=(_xtb,), writes=(maskB_b,))
    dma("sp", sel8[0:8, :], d_sel8[:, :], (), (sel8_b,), "dx")
    _xt, _xtb = xt[0]
    dma("sp", _xt[:, 0:64], d_onesel[:, :], (), (_xtb,), "dx")
    S.op("dve", lambda h, _xt=_xt: h.tensor_copy(out=onesel[:, :],
                                                 in_=_xt[:, 0:64]),
         reads=(_xtb,), writes=(onesel_b,))

    def load_wT_split(dW, rows0, wdst):
        for r in range(4):
            ws, wsb = wstage[0]
            dma("sp", ws[:, :], dW[rows0 + 128 * r:rows0 + 128 * r + 128, :],
                (), (wsb,), "dx")
            for cc in range(NCH):
                bank = 4 + (cc % 2)
                transpose_bf(ws[:, 128 * cc:128 * cc + 128], wsb, bank,
                             wdst[0][0], wdst[0][1], wdst[1][0], wdst[1][1],
                             512 * cc + 128 * r)

    def xnT_tile(xnt, xnb, t):
        for ccg in range(2):
            bank = (2 * (t % 2) + ccg) % 4
            pt, ptb = pb[bank], pbuf[bank]
            for q in range(4):
                cc = 4 * ccg + q
                S.op("pe", lambda h, pt=pt, q=q, cc=cc, xnt=xnt: h.transpose(
                    pt[:, 128 * q:128 * q + 128],
                    xnt[:, 128 * cc:128 * cc + 128], ident[:, :]),
                    reads=(xnb, ident_b), writes=(ptb,))
            base = 512 * 4 * ccg + 128 * t
            oap_h = bass.AP(xnTh, base, [[4096, 128], [512, 4], [1, 128]])
            oap_l = bass.AP(xnTl, base, [[4096, 128], [512, 4], [1, 128]])
            iap = bass.AP(pb[bank], 0, [[512, 128], [128, 4], [1, 128]])
            S.op("act", lambda h, oap_h=oap_h, iap=iap: h.activation(
                out=oap_h, in_=iap, func=AF.Copy),
                reads=(ptb,), writes=(xnTh_b,))
            S.op("dve", lambda h, oap_l=oap_l, iap=iap, oap_h=oap_h:
                 h.tensor_tensor(oap_l, iap, oap_h, op=ALU.subtract),
                 reads=(ptb, xnTh_b), writes=(xnTl_b,))

    def gen_xnT(x_dram, row0, which_bufs=None):
        lnres = []
        for t in range(4):
            i = t % 2
            xtt, xtb = xt[i]
            dma("sp", xtt[:, :], x_dram[row0 + 128 * t:row0 + 128 * t + 128, :],
                (), (xtb,), "dx")
            lnres.append(ln_tile(i, xtt, xtb))
            if t >= 1:
                xnt, xnb = lnres[t - 1]
                xnT_tile(xnt, xnb, t - 1)
        xnt, xnb = lnres[3]
        xnT_tile(xnt, xnb, 3)

    def proj_pair(wT, dsth, dsthb, dstl, dstlb, dcol, pair, n, bank):
        bk, bkb = pb[bank], pbuf[bank]
        for cc in range(NCH):
            lh = wT[0][0][:, 512 * cc + 128 * pair:512 * cc + 128 * pair + 128]
            ll = wT[1][0][:, 512 * cc + 128 * pair:512 * cc + 128 * pair + 128]
            mm3(bk[:, 0:n], bkb, lh, ll, xnTh[:, 512 * cc:512 * cc + n],
                xnTl[:, 512 * cc:512 * cc + n],
                cc == 0, cc == NCH - 1, (wT[0][1], wT[1][1], xnTh_b, xnTl_b))
        S.op("act", lambda h: h.activation(out=dsth[:, dcol:dcol + n],
                                           in_=bk[:, 0:n], func=AF.Copy),
             reads=(bkb,), writes=(dsthb,))
        S.op("dve", lambda h: h.tensor_tensor(dstl[:, dcol:dcol + n], bk[:, 0:n],
                                              dsth[:, dcol:dcol + n],
                                              op=ALU.subtract),
             reads=(bkb, dsthb), writes=(dstlb,))

    ngroups = 2
    for g2 in range(ngroups):
        load_wT_split(d_wq, 512 * g2, wA)
        for s in range(4):
            gen_xnT(d_xq, 512 * s)
            if stage == "xnq" and g2 == 0 and s == 0:
                for cc in range(2):
                    dump_f32(2 * cc, xnT[0][cc][0][:, :], xnT[0][cc][1], 512)
                    dump_f32(2 * cc + 1, xnT[1][cc][0][:, :], xnT[1][cc][1], 512)
                break
            for p in range(4):
                proj_pair(wA, qT[0][p][0], qT[0][p][1], qT[1][p][0], qT[1][p][1],
                          512 * s, p, 512, 6 + (p % 2))
        if stage == "xnq":
            break
        if stage == "qkv" and g2 == 0:
            dump_f32(0, qT[0][0][0][:, :], qT[0][0][1], 2048)
            dump_f32(1, qT[1][0][0][:, :], qT[1][0][1], 2048)
            dump_f32(2, qT[0][3][0][:, :], qT[0][3][1], 2048)
            dump_f32(3, qT[1][3][0][:, :], qT[1][3][1], 2048)
            break

        load_wT_split(d_wk, 512 * g2, wA)
        load_wT_split(d_wv, 512 * g2, wB)

        def kv_chunks(jn):
            par = jn % 2
            chunks = []

            lnout = {}

            def mk_ln(t):
                def em():
                    i = t % 2
                    xtt, xtb = xt[i]
                    dma("sp", xtt[:, :],
                        d_xkv[512 * jn + 128 * t:512 * jn + 128 * t + 128, :],
                        (), (xtb,), "dx")
                    lnout[t] = ln_tile(i, xtt, xtb)
                return em

            def mk_tr(t):
                def em():
                    xnt, xnb = lnout[t]
                    xnT_tile(xnt, xnb, t)
                return em

            def mk_kpair(p):
                def em():
                    proj_pair(wA, kT[0][p][par][0], kT[0][p][par][1],
                              kT[1][p][par][0], kT[1][p][par][1],
                              0, p, 512, 6)
                return em

            def mk_vchunk(m):
                def em():
                    bk, bkb = pb[6], pbuf[6]
                    for cc in range(NCH):
                        mm3(bk[:, :], bkb,
                            xnTh[:, 512 * cc + 128 * m:
                                 512 * cc + 128 * m + 128],
                            xnTl[:, 512 * cc + 128 * m:
                                 512 * cc + 128 * m + 128],
                            wB[0][0][:, 512 * cc:512 * cc + 512],
                            wB[1][0][:, 512 * cc:512 * cc + 512],
                            cc == 0, cc == NCH - 1,
                            (xnTh_b, xnTl_b, wB[0][1], wB[1][1]))
                    vh_t, vh_b = vv[0][par]
                    vl_t, vl_b = vv[1][par]
                    S.op("act", lambda h, bk=bk, vh_t=vh_t: h.activation(
                        out=vh_t[:, 512 * m:512 * m + 512], in_=bk[:, :],
                        func=AF.Copy),
                        reads=(bkb,), writes=(vh_b,))
                    S.op("dve", lambda h, bk=bk, vh_t=vh_t, vl_t=vl_t:
                         h.tensor_tensor(
                             vl_t[:, 512 * m:512 * m + 512], bk[:, :],
                             vh_t[:, 512 * m:512 * m + 512], op=ALU.subtract),
                         reads=(bkb, vh_b), writes=(vl_b,))
                return em

            chunks.append(mk_ln(0))
            chunks.append(mk_ln(1))
            chunks.append(mk_tr(0))
            chunks.append(mk_ln(2))
            chunks.append(mk_tr(1))
            chunks.append(mk_ln(3))
            chunks.append(mk_tr(2))
            chunks.append(mk_tr(3))
            for i in range(4):
                chunks.append(mk_kpair(i))
                chunks.append(mk_vchunk(i))
            return chunks

        for ch in kv_chunks(0):
            ch()
        for j in range(NKV):
            atoms = []
            for c in range(4):
                if j > 2 * c + 1:
                    continue
                if j == 2 * c + 1:
                    qt0, qn, msk = 512 * c + 256, 256, "B"
                elif j == 2 * c:
                    qt0, qn, msk = 512 * c, 512, "A"
                else:
                    qt0, qn, msk = 512 * c, 512, None
                smb, sm_bb = pb[7], pbuf[7]
                def mk_pre(p, m, qt0=None, qn=None, msk=None, j=None):
                    def emit_pre():
                        be, beb = pb[2 * (m % 2)], pbuf[2 * (m % 2)]
                        bo_, bob = pb[2 * (m % 2) + 1], pbuf[2 * (m % 2) + 1]
                        for (kw, qw, frst, lst) in ((0, 0, True, False),
                                                    (0, 1, False, False),
                                                    (1, 0, False, True)):
                            kh, khb = kT[kw][p][j % 2]
                            qh, qhb = qT[qw][p]
                            S.op("pe", lambda h, kh=kh, qh=qh, be=be,
                                 frst=frst, lst=lst: h.matmul(
                                     be[:, 0:qn],
                                     kh[0:64, 128 * m:128 * m + 128],
                                     qh[0:64, qt0:qt0 + qn],
                                     start=frst, stop=lst,
                                     tile_position=(0, 0)),
                                 reads=(khb, qhb), writes=(beb,))
                            S.op("pe", lambda h, kh=kh, qh=qh, bo_=bo_,
                                 frst=frst, lst=lst: h.matmul(
                                     bo_[:, 0:qn],
                                     kh[64:128, 128 * m:128 * m + 128],
                                     qh[64:128, qt0:qt0 + qn],
                                     start=frst, stop=lst,
                                     tile_position=(64, 0)),
                                 reads=(khb, qhb), writes=(bob,))
                        if msk == "A":
                            for bx, bxb in ((be, beb), (bo_, bob)):
                                S.op("dve", lambda h, bx=bx: h.tensor_tensor(
                                    bx[:, 0:512], bx[:, 0:512],
                                    maskA[:, 512 * m:512 * m + 512],
                                    op=ALU.add),
                                    reads=(bxb, maskA_b), writes=(bxb,))
                        elif msk == "B":
                            for bx, bxb in ((be, beb), (bo_, bob)):
                                S.op("dve", lambda h, bx=bx: h.tensor_tensor(
                                    bx[:, 0:256], bx[:, 0:256],
                                    maskB[:, 256 * m:256 * m + 256],
                                    op=ALU.add),
                                    reads=(bxb, maskB_b), writes=(bxb,))
                        for hh, (bx, bxb) in enumerate(((be, beb), (bo_, bob))):
                            pidx = 2 * (m % 2) + hh
                            p32t, p32b = p32[pidx]
                            pht, phb = ph[pidx]
                            plt, plb = pl[pidx]
                            S.op("act", lambda h, bx=bx, p32t=p32t:
                                 h.activation(out=p32t[:, 0:qn],
                                              in_=bx[:, 0:qn],
                                              func=AF.Exp, scale=QK_SCALE),
                                 reads=(bxb,), writes=(p32b,))
                            S.op("dve", lambda h, p32t=p32t, pht=pht:
                                 h.tensor_copy(out=pht[:, 0:qn],
                                               in_=p32t[:, 0:qn]),
                                 reads=(p32b,), writes=(phb,))
                            S.op("dve", lambda h, p32t=p32t, pht=pht, plt=plt:
                                 h.tensor_tensor(
                                     plt[:, 0:qn], p32t[:, 0:qn], pht[:, 0:qn],
                                     op=ALU.subtract),
                                 reads=(p32b, phb), writes=(plb,))
                    return emit_pre

                def mk_post(p, m, qt0=None, qn=None, j=None):
                    def emit_post():
                        for ti, (vw, pw) in enumerate(((0, 0), (0, 1), (1, 0))):
                            vt, vtb = vv[vw][j % 2]
                            for hh in range(2):
                                pt, ptb = (ph if pw == 0
                                           else pl)[2 * (m % 2) + hh]
                                vcol = 512 * m + 128 * p + 64 * hh
                                S.op("pe", lambda h, vt=vt, pt=pt, vcol=vcol,
                                     hh=hh, bk=av_bk[hh],
                                     st=(m == 0 and ti == 0),
                                     sp=(m == 3 and ti == 2): h.matmul(
                                         bk[64 * hh:64 * hh + 64, 0:qn],
                                         vt[:, vcol:vcol + 64], pt[:, 0:qn],
                                         start=st, stop=sp,
                                         tile_position=(0, 64 * hh)),
                                     reads=(vtb, ptb), writes=(av_bb2[hh],))
                        for hh in range(2):
                            q8 = 2 * p + hh
                            for pw in range(2):
                                pt, ptb = (ph if pw == 0
                                           else pl)[2 * (m % 2) + hh]
                                S.op("pe", lambda h, q8=q8, pt=pt,
                                     st=(p == 0 and m == 0 and hh == 0
                                         and pw == 0),
                                     sp=(p == 3 and m == 3 and hh == 1
                                         and pw == 1):
                                     h.matmul(smb[0:8, 0:qn],
                                              onesel[:, 8 * q8:8 * q8 + 8],
                                              pt[:, 0:qn], start=st, stop=sp),
                                     reads=(onesel_b, ptb), writes=(sm_bb,))
                        if m == 3:
                            a_t, a_b = acc[p]
                            for hh in range(2):
                                r0, r1 = 64 * hh, 64 * hh + 64
                                bk = av_bk[hh]
                                if j == 0:
                                    S.op("dve", lambda h, a_t=a_t, bk=bk,
                                         r0=r0, r1=r1:
                                         h.tensor_copy(
                                             out=a_t[r0:r1, qt0:qt0 + qn],
                                             in_=bk[r0:r1, 0:qn]),
                                         reads=(av_bb2[hh],), writes=(a_b,))
                                else:
                                    S.op("dve", lambda h, a_t=a_t, bk=bk,
                                         r0=r0, r1=r1:
                                         h.tensor_tensor(
                                             a_t[r0:r1, qt0:qt0 + qn],
                                             bk[r0:r1, 0:qn],
                                             a_t[r0:r1, qt0:qt0 + qn],
                                             op=ALU.add),
                                         reads=(av_bb2[hh], a_b),
                                         writes=(a_b,))
                    return emit_post

                av_bk = (pb[4], pb[5])
                av_bb2 = (pbuf[4], pbuf[5])

                def mk_sums_evac(qt0, qn, j):
                    def em():
                        if j == 0:
                            S.op("dve", lambda h: h.tensor_copy(
                                out=sums[0:8, qt0:qt0 + qn],
                                in_=smb[0:8, 0:qn]),
                                reads=(sm_bb,), writes=(sums_b,))
                        else:
                            S.op("dve", lambda h: h.tensor_tensor(
                                sums[0:8, qt0:qt0 + qn], smb[0:8, 0:qn],
                                sums[0:8, qt0:qt0 + qn], op=ALU.add),
                                reads=(sm_bb, sums_b), writes=(sums_b,))
                    return em

                for p in range(4):
                    for m in range(4):
                        atoms.append((mk_pre(p, m, qt0, qn, msk, j),
                                      mk_post(p, m, qt0, qn, j),
                                      mk_sums_evac(qt0, qn, j)
                                      if (p == 3 and m == 3) else None))

            nxt = kv_chunks(j + 1) if j + 1 < NKV else []
            n = len(atoms)
            done_ch = 0
            atoms[0][0]()
            for k in range(1, n):
                atoms[k][0]()
                atoms[k - 1][1]()
                if atoms[k - 1][2]:
                    atoms[k - 1][2]()
                want = (k * len(nxt)) // n
                while done_ch < want:
                    nxt[done_ch]()
                    done_ch += 1
            atoms[-1][1]()
            if atoms[-1][2]:
                atoms[-1][2]()
            while done_ch < len(nxt):
                nxt[done_ch]()
                done_ch += 1

        S.op("dve", lambda h: h.reciprocal(sums[0:8, :], sums[0:8, :]),
             reads=(sums_b,), writes=(sums_b,), drain_before=True, drain=True)
        for p in range(4):
            a_t, a_b = acc[p]
            for ch in range(4):
                bk, bkb = pb[6], pbuf[6]
                S.op("pe", lambda h, p=p, ch=ch, bk=bk: h.matmul(
                    bk[:, 0:512], sel8[0:8, 128 * p:128 * p + 128],
                    sums[0:8, 512 * ch:512 * ch + 512],
                    start=True, stop=True),
                    reads=(sel8_b, sums_b), writes=(bkb,))
                S.op("dve", lambda h, a_t=a_t, ch=ch, bk=bk: h.tensor_tensor(
                    a_t[:, 512 * ch:512 * ch + 512],
                    a_t[:, 512 * ch:512 * ch + 512], bk[:, 0:512], op=ALU.mult),
                    reads=(a_b, bkb), writes=(a_b,))
        if stage == "attn":
            for p in range(4):
                dump_f32(p, acc[p][0][:, :], acc[p][1], 2048)
            break

        for pn in range(2):
            for r in range(4):
                for bc in range(4):
                    bcol = 4 * pn + bc
                    ws, wsb = wstage[0]
                    dma("sp", ws[:, 0:128],
                        d_wp[128 * bcol:128 * bcol + 128,
                             512 * g2 + 128 * r:512 * g2 + 128 * r + 128],
                        (), (wsb,), "dx")
                    transpose_f32(ws[:, 0:128], wsb, 4 + ((r + bc) % 2),
                                  wpT32[:, 512 * r + 128 * bc:
                                        512 * r + 128 * bc + 128], wpT32_b)
            for tt in range(NT):
                i = tt % 2
                xtt, xtb = xt[i]
                if g2 == 0:
                    dma("sp", xtt[:, 512 * pn:512 * pn + 512],
                        d_xq[128 * tt:128 * tt + 128,
                             512 * pn:512 * pn + 512],
                        (), (xtb,), "dx")
                else:
                    dma("sp", xtt[:, 512 * pn:512 * pn + 512],
                        d_x2[128 * tt:128 * tt + 128,
                             512 * pn:512 * pn + 512],
                        (x2_half[tt][pn],), (xtb,), "dx")
                ot, ob = x2o[0]
                bk, bkb = pb[6 + (tt % 2)], pbuf[6 + (tt % 2)]
                for p in range(4):
                    a_t, a_b = acc[p]
                    S.op("pe", lambda h, a_t=a_t, p=p, tt=tt, bk=bk:
                         h.matmul(bk[:, 0:512],
                                  a_t[:, 128 * tt:128 * tt + 128],
                                  wpT32[:, 512 * p:512 * p + 512],
                                  start=(p == 0), stop=(p == 3)),
                         reads=(a_b, wpT32_b), writes=(bkb,))
                S.op("dve", lambda h, ot=ot, pn=pn, bk=bk, xtt=xtt:
                     h.tensor_tensor(ot[:, 512 * pn:512 * pn + 512],
                                     bk[:, 0:512],
                                     xtt[:, 512 * pn:512 * pn + 512],
                                     op=ALU.add),
                     reads=(bkb, xtb), writes=(ob,))
                dma("gp", d_x2[128 * tt:128 * tt + 128, 512 * pn:512 * pn + 512],
                    ot[:, 512 * pn:512 * pn + 512], (ob,),
                    (x2_half[tt][pn],), "dg")

    astack.close()

    if stage in ("full", "ffn1"):
        S.barrier()
        fstack = ExitStack()
        sbf = mksb(fstack)
        wm32, wm32_b = sbf("wm32", [128, 8192], F32)
        wos = [sbf("wos0", [128, 1024], F32)]
        emb32, emb32_b = sbf("emb32", [128, NEXP], F32)
        sel16, sel16_b = sbf("sel16_s", [16, 2048], BF16)
        b2T, b2T_b = sbf("b2T", [16, 128], F32)
        xn2T = [sbf(f"xn2T{cc}", [128, 512], F32) for cc in range(NCH)]
        hT32, hT32_b = sbf("hT32", [128, 4096], F32)
        tTh, tTh_b = sbf("tTh", [128, 4096], BF16)
        tTl, tTl_b = sbf("tTl", [128, 4096], BF16)
        lg, lg_b = sbf("lg", [128, 128], F32)
        ew4, ew4_b = sbf("ew4", [128, 64], F32)
        aux4, aux4_b = sbf("aux4", [128, 64], F32)
        nm4, nm4_b = sbf("nm4", [128, 4], F32)
        sc4, sc4_b = sbf("sc4", [128, 4], F32)
        m14, m14_b = sbf("m14", [128, 4], F32)
        cmat4 = [sbf(f"cmat{i}", [128, 128], F32) for i in range(2)]
        cTh = [sbf(f"cTh{i}", [16, 4096], BF16) for i in range(2)]
        cTl = [sbf(f"cTl{i}", [16, 4096], BF16) for i in range(2)]
        b2h, b2h_b = sbf("b2h", [16, 128], BF16)
        b2l, b2l_b = sbf("b2l", [16, 128], BF16)
        w1s = [sbf(f"w1s{i}", [128, FI], F32) for i in range(2)]
        w1h = [sbf(f"w1h{i}", [128, FI], BF16) for i in range(2)]
        w1l = [sbf(f"w1l{i}", [128, FI], BF16) for i in range(2)]
        w2s = [sbf(f"w2s{i}", [128, FI], F32) for i in range(2)]
        b1s = [sbf(f"b1s{i}", [128, 4], F32) for i in range(2)]
        bms, bms_b = sbf("bms", [128, 8], F32)
        bos, bos_b = sbf("bos", [1, D], F32)
        bobc, bobc_b = sbf("bobc", [128, D], F32)
        onesr, onesr_b = sbf("onesr", [1, 128], F32)
        hmid = [sbf(f"hmid{i}", [128, 2048], F32) for i in range(2)]
        comb, comb_b = sbf("comb", [128, 4096], F32)
        cbc, cbc_b = sbf("cbc", [128, 512], F32)
        tmp1, tmp1_b = sbf("tmp1", [128, 512], F32)

        dma("sp", bms[:, :], d_bm[:, :], (), (bms_b,), "dx")
        dma("sp", bos[0:1, :], d_bo[0:1, :], (), (bos_b,), "dx")
        dma("sp", emb32[:, :], d_emb[:, :], (), (emb32_b,), "dx")
        _st, _stb = hmid[0]
        dma("sp", _st[0:16, :], d_sel16[:, :], (), (_stb,), "dx")
        S.op("dve", lambda h, _st=_st: h.tensor_copy(out=sel16[0:16, :],
                                                     in_=_st[0:16, :]),
             reads=(_stb,), writes=(sel16_b,))
        dma("sp", b2T[0:16, :], d_b2[:, :], (), (b2T_b,), "dx")
        S.op("dve", lambda h: h.tensor_copy(out=b2h[0:16, :], in_=b2T[0:16, :]),
             reads=(b2T_b,), writes=(b2h_b,))
        S.op("dve", lambda h: h.tensor_tensor(b2l[0:16, :], b2T[0:16, :],
                                              b2h[0:16, :], op=ALU.subtract),
             reads=(b2T_b, b2h_b), writes=(b2l_b,))
        for r in range(NCH):
            dma("sp", wm32[:, 1024 * r:1024 * r + 1024],
                d_wm[128 * r:128 * r + 128, :], (), (wm32_b,), "dx")
        S.op("dve", lambda h: h.memset(onesr[0:1, :], 1.0), reads=(),
             writes=(onesr_b,))
        for chn in range(2):
            bk, bkb = pb[6], pbuf[6]
            S.op("pe", lambda h, chn=chn, bk=bk: h.matmul(
                bk[:, 0:512], onesr[0:1, :], bos[0:1, 512 * chn:512 * chn + 512],
                start=True, stop=True),
                reads=(onesr_b, bos_b), writes=(bkb,))
            S.op("dve", lambda h, chn=chn, bk=bk: h.tensor_copy(
                out=bobc[:, 512 * chn:512 * chn + 512], in_=bk[:, 0:512]),
                reads=(bkb,), writes=(bobc_b,))

        def prologue_chunks(sx):
            par = sx % 2
            cth_t, cth_b = cTh[par]
            ctl_t, ctl_b = cTl[par]
            chunks = []

            def mk_tile(t):
                def em():
                    x2tt, x2tb = xt[t % 2]
                    dma("sp", x2tt[:, :],
                        d_x2[512 * sx + 128 * t:512 * sx + 128 * t + 128, :],
                        (x2_half[4 * sx + t][0], x2_half[4 * sx + t][1]),
                        (x2tb,), "dx")
                    xnt, xnb = ln_tile(t % 2, x2tt, x2tb)
                    for cc in range(NCH):
                        dt_t, dt_b = xn2T[cc]
                        transpose_f32(xnt[:, 128 * cc:128 * cc + 128], xnb,
                                      cc % 4, dt_t[:, 128 * t:128 * t + 128],
                                      dt_b)
                return em

            def mk_wm(h0):
                def em():
                    for hc in (h0, h0 + 1):
                        bk, bkb = pb[6 + (hc % 2)], pbuf[6 + (hc % 2)]
                        for cc in range(NCH):
                            xt_t, xt_b2 = xn2T[cc]
                            S.op("pe", lambda h, hc=hc, cc=cc, bk=bk,
                                 xt_t=xt_t: h.matmul(
                                     bk[:, 0:512],
                                     wm32[:, 1024 * cc + 128 * hc:
                                          1024 * cc + 128 * hc + 128],
                                     xt_t[:, 0:512], start=(cc == 0),
                                     stop=(cc == NCH - 1)),
                                 reads=(wm32_b, xt_b2), writes=(bkb,))
                        S.op("dve", lambda h, hc=hc, bk=bk: h.tensor_scalar(
                            hT32[:, 512 * hc:512 * hc + 512], bk[:, 0:512],
                            bms[:, hc:hc + 1], None, op0=ALU.add),
                            reads=(bkb, bms_b), writes=(hT32_b,))
                return em

            def mk_router(tm):
                cmat, cmat_b = cmat4[tm % 2]
                def em():
                    bk, bkb = pb[tm % 2], pbuf[tm % 2]
                    for jj in range(NJ):
                        S.op("pe", lambda h, jj=jj, bk=bk: h.matmul(
                            bk[:, 16 * jj:16 * jj + 16],
                            hT32[:, 512 * jj + 128 * tm:
                                 512 * jj + 128 * tm + 128],
                            emb32[:, :], start=True, stop=True),
                            reads=(hT32_b, emb32_b), writes=(bkb,))
                    S.op("dve", lambda h, bk=bk: h.tensor_copy(
                        out=lg[:, :], in_=bk[:, 0:128]),
                        reads=(bkb,), writes=(lg_b,))
                    for jb in range(0, NJ, 4):
                        def sl(q, t16):
                            return t16[:, 16 * q:16 * q + 16]
                        for q in range(4):
                            S.op("dve", lambda h, q=q, jj=jb + q: h.reduce_max(
                                nm4[:, q:q + 1], lg[:, 16 * jj:16 * jj + 16],
                                axis=AX.X, negate=True),
                                reads=(lg_b,), writes=(nm4_b,),
                                drain_before=(q == 0), drain=(q == 3))
                        for q in range(4):
                            S.op("act", lambda h, q=q, jj=jb + q: h.activation(
                                out=sl(q, ew4), in_=lg[:, 16 * jj:16 * jj + 16],
                                func=AF.Exp, bias=nm4[:, q:q + 1]),
                                reads=(lg_b, nm4_b), writes=(ew4_b,))
                        for q in range(4):
                            S.op("dve", lambda h, q=q: h.reduce_sum(
                                sc4[:, q:q + 1], sl(q, ew4), axis=AX.X),
                                reads=(ew4_b,), writes=(sc4_b,),
                                drain_before=(q == 0), drain=(q == 3))
                        for q in range(4):
                            S.op("dve", lambda h, q=q: h.reciprocal(
                                sc4[:, q:q + 1], sc4[:, q:q + 1]),
                                reads=(sc4_b,), writes=(sc4_b,),
                                drain_before=(q == 0), drain=(q == 3))
                        for q in range(4):
                            S.op("dve", lambda h, q=q: h.reduce_max(
                                m14[:, q:q + 1], sl(q, ew4), axis=AX.X),
                                reads=(ew4_b,), writes=(m14_b,),
                                drain_before=(q == 0), drain=(q == 3))
                        for q in range(4):
                            S.op("dve", lambda h, q=q: h.tensor_scalar(
                                sl(q, aux4), sl(q, ew4), m14[:, q:q + 1], None,
                                op0=ALU.is_equal),
                                reads=(ew4_b, m14_b), writes=(aux4_b,),
                                drain_before=(q == 0))
                        for q in range(4):
                            S.op("dve", lambda h, q=q: h.scalar_tensor_tensor(
                                sl(q, aux4), sl(q, aux4), -2.0, sl(q, ew4),
                                op0=ALU.mult, op1=ALU.add),
                                reads=(aux4_b, ew4_b), writes=(aux4_b,),
                                drain_before=(q == 0))
                        for q in range(4):
                            S.op("dve", lambda h, q=q: h.reduce_max(
                                m14[:, q:q + 1], sl(q, aux4), axis=AX.X),
                                reads=(aux4_b,), writes=(m14_b,),
                                drain_before=(q == 0), drain=(q == 3))
                        for q in range(4):
                            S.op("dve", lambda h, q=q: h.tensor_scalar(
                                sl(q, aux4), sl(q, ew4), m14[:, q:q + 1], None,
                                op0=ALU.is_ge),
                                reads=(ew4_b, m14_b), writes=(aux4_b,),
                                drain_before=(q == 0))
                        for q in range(4):
                            S.op("dve", lambda h, q=q, jj=jb + q:
                                 h.scalar_tensor_tensor(
                                     cmat[:, 16 * jj:16 * jj + 16], sl(q, ew4),
                                     sc4[:, q:q + 1], sl(q, aux4),
                                     op0=ALU.mult, op1=ALU.mult),
                                 reads=(ew4_b, sc4_b, aux4_b), writes=(cmat_b,),
                                 drain_before=(q == 0))
                return em

            def mk_ctrans(tm, cm_t, cm_b):
                def em():
                    for jj in range(NJ):
                        bk2, bk2b = pb[2 + (jj % 2)], pbuf[2 + (jj % 2)]
                        S.op("pe", lambda h, jj=jj, bk2=bk2, cm_t=cm_t:
                             h.transpose(
                                 bk2[0:16, 0:128],
                                 cm_t[:, 16 * jj:16 * jj + 16], ident[:, :]),
                             reads=(cm_b, ident_b), writes=(bk2b,))
                        col = 512 * jj + 128 * tm
                        S.op("act", lambda h, bk2=bk2, col=col: h.activation(
                            out=cth_t[0:16, col:col + 128],
                            in_=bk2[0:16, 0:128], func=AF.Copy),
                            reads=(bk2b,), writes=(cth_b,))
                        S.op("dve", lambda h, bk2=bk2, col=col: h.tensor_tensor(
                            ctl_t[0:16, col:col + 128], bk2[0:16, 0:128],
                            cth_t[0:16, col:col + 128], op=ALU.subtract),
                            reads=(bk2b, cth_b), writes=(ctl_b,))
                return em

            for t in range(4):
                chunks.append(mk_tile(t))
            for h0 in (0, 2, 4, 6):
                chunks.append(mk_wm(h0))
            rpend = []
            for tm in range(4):
                chunks.append(mk_router(tm))
                rpend.append(mk_ctrans(tm, *cmat4[tm % 2]))
                if tm >= 1:
                    chunks.append(rpend[tm - 1])
            chunks.append(rpend[3])
            return chunks

        def tT_split(sx):
            for hc in range(NCH):
                S.op("act", lambda h, hc=hc: h.activation(
                    out=tTh[:, 512 * hc:512 * hc + 512],
                    in_=hT32[:, 512 * hc:512 * hc + 512], func=AF.Copy),
                    reads=(hT32_b,), writes=(tTh_b,))
                S.op("dve", lambda h, hc=hc: h.tensor_tensor(
                    tTl[:, 512 * hc:512 * hc + 512],
                    hT32[:, 512 * hc:512 * hc + 512],
                    tTh[:, 512 * hc:512 * hc + 512], op=ALU.subtract),
                    reads=(hT32_b, tTh_b), writes=(tTl_b,))

        for ch in prologue_chunks(0):
            ch()
        tT_split(0)
        for s in range(4):
            par = s % 2
            cth_t, cth_b = cTh[par]
            ctl_t, ctl_b = cTl[par]
            nxt = prologue_chunks(s + 1) if s < 3 else []
            for e in range(NEXP):
                w1st, w1sb = w1s[e % 2]
                w1ht, w1hb = w1h[e % 2]
                w1lt, w1lb = w1l[e % 2]
                w2t, w2b = w2s[e % 2]
                b1t, b1b = b1s[e % 2]
                dma("sp", w1st[:, :], d_w1[e, :, :], (), (w1sb,), "dx")
                S.op("dve", lambda h, w1st=w1st, w1ht=w1ht: h.tensor_copy(
                    out=w1ht[:, :], in_=w1st[:, :]),
                    reads=(w1sb,), writes=(w1hb,))
                S.op("dve", lambda h, w1st=w1st, w1ht=w1ht, w1lt=w1lt:
                     h.tensor_tensor(w1lt[:, :], w1st[:, :], w1ht[:, :],
                                     op=ALU.subtract),
                     reads=(w1sb, w1hb), writes=(w1lb,))
                dma("sp", w2t[:, :], d_w2r[e, :, :], (), (w2b,), "dx")
                dma("sp", b1t[:, :], d_b1[e, :, :], (), (b1b,), "dx")
                for jj in range(NJ):
                    hm_t, hm_b = hmid[jj % 2]
                    for ic in range(4):
                        bk, bkb = pb[ic % 2], pbuf[ic % 2]
                        mm3(bk[:, 0:512], bkb,
                            w1ht[:, 128 * ic:128 * ic + 128],
                            w1lt[:, 128 * ic:128 * ic + 128],
                            tTh[:, 512 * jj:512 * jj + 512],
                            tTl[:, 512 * jj:512 * jj + 512],
                            True, True, (w1hb, w1lb, tTh_b, tTl_b))
                        S.op("act", lambda h, ic=ic, bk=bk, hm_t=hm_t, b1t=b1t:
                             h.activation(out=hm_t[:, 512 * ic:512 * ic + 512],
                                          in_=bk[:, 0:512], func=AF.Gelu,
                                          bias=b1t[:, ic:ic + 1]),
                             reads=(bkb, b1b), writes=(hm_b,))
                    bke, bkeb = pb[2 + (jj % 2)], pbuf[2 + (jj % 2)]
                    for ic in range(4):
                        S.op("pe", lambda h, ic=ic, bke=bke, w2t=w2t, hm_t=hm_t:
                             h.matmul(bke[:, 0:512],
                                      w2t[:, 128 * ic:128 * ic + 128],
                                      hm_t[:, 512 * ic:512 * ic + 512],
                                      start=(ic == 0), stop=(ic == 3)),
                             reads=(w2b, hm_b), writes=(bkeb,))
                    bkc, bkcb = pb[4], pbuf[4]
                    S.op("pe", lambda h, e=e, jj=jj, bkc=bkc, cth_t=cth_t:
                         h.matmul(
                             bkc[:, 0:512], sel16[0:16, 128 * e:128 * e + 128],
                             cth_t[0:16, 512 * jj:512 * jj + 512], start=True,
                             stop=False),
                         reads=(sel16_b, cth_b), writes=(bkcb,))
                    S.op("pe", lambda h, e=e, jj=jj, bkc=bkc, ctl_t=ctl_t:
                         h.matmul(
                             bkc[:, 0:512], sel16[0:16, 128 * e:128 * e + 128],
                             ctl_t[0:16, 512 * jj:512 * jj + 512], start=False,
                             stop=True),
                         reads=(sel16_b, ctl_b), writes=(bkcb,))
                    S.op("act", lambda h, bkc=bkc: h.activation(
                        out=cbc[:, 0:512], in_=bkc[:, 0:512], func=AF.Copy),
                        reads=(bkcb,), writes=(cbc_b,))
                    if e == 0:
                        S.op("dve", lambda h, jj=jj, bke=bke: h.tensor_tensor(
                            comb[:, 512 * jj:512 * jj + 512], bke[:, 0:512],
                            cbc[:, 0:512], op=ALU.mult),
                            reads=(bkeb, cbc_b), writes=(comb_b,))
                    else:
                        S.op("dve", lambda h, bke=bke: h.tensor_tensor(
                            tmp1[:, 0:512], bke[:, 0:512], cbc[:, 0:512],
                            op=ALU.mult),
                            reads=(bkeb, cbc_b), writes=(tmp1_b,))
                        S.op("dve", lambda h, jj=jj: h.tensor_tensor(
                            comb[:, 512 * jj:512 * jj + 512], tmp1[:, 0:512],
                            comb[:, 512 * jj:512 * jj + 512], op=ALU.add),
                            reads=(tmp1_b, comb_b), writes=(comb_b,))
                if nxt and 1 <= e <= len(nxt):
                    nxt[e - 1]()
            for k in range(NEXP - 1, len(nxt)):
                nxt[k]()
            for jj in range(NJ):
                bk, bkb = pb[5], pbuf[5]
                mm3(bk[:, 0:512], bkb, b2h[0:16, :], b2l[0:16, :],
                    cth_t[0:16, 512 * jj:512 * jj + 512],
                    ctl_t[0:16, 512 * jj:512 * jj + 512],
                    True, True, (b2h_b, b2l_b, cth_b, ctl_b))
                S.op("dve", lambda h, jj=jj, bk=bk: h.tensor_tensor(
                    comb[:, 512 * jj:512 * jj + 512], bk[:, 0:512],
                    comb[:, 512 * jj:512 * jj + 512], op=ALU.add),
                    reads=(bkb, comb_b), writes=(comb_b,))
            for dc in range(NCH):
                wot, wob = wos[0]
                dma("sp", wot[:, :], d_wo[128 * dc:128 * dc + 128, :],
                    (), (wob,), "dx")
                for tm in range(4):
                    for pn in range(2):
                        bk, bkb = pb[2 * tm + pn], pbuf[2 * tm + pn]
                        S.op("pe", lambda h, dc=dc, tm=tm, pn=pn, bk=bk,
                             wot=wot: h.matmul(
                                 bk[:, 0:512],
                                 comb[:, 512 * dc + 128 * tm:
                                      512 * dc + 128 * tm + 128],
                                 wot[:, 512 * pn:512 * pn + 512],
                                 start=(dc == 0), stop=(dc == NCH - 1)),
                             reads=(comb_b, wob), writes=(bkb,))
            for tm in range(4):
                x2tt, x2tb = xt[tm % 2]
                dma("sp", x2tt[:, :],
                    d_x2[512 * s + 128 * tm:512 * s + 128 * tm + 128, :],
                    (x2_half[4 * s + tm][0], x2_half[4 * s + tm][1]),
                    (x2tb,), "dx")
                ot, ob = x2o[0]
                for pn in range(2):
                    bk, bkb = pb[2 * tm + pn], pbuf[2 * tm + pn]
                    S.op("dve", lambda h, pn=pn, bk=bk: h.tensor_tensor(
                        tmp1[:, 0:512], bk[:, 0:512],
                        bobc[:, 512 * pn:512 * pn + 512], op=ALU.add),
                        reads=(bkb, bobc_b), writes=(tmp1_b,))
                    S.op("dve", lambda h, ot=ot, pn=pn, x2tt=x2tt:
                         h.tensor_tensor(ot[:, 512 * pn:512 * pn + 512],
                                         tmp1[:, 0:512],
                                         x2tt[:, 512 * pn:512 * pn + 512],
                                         op=ALU.add),
                         reads=(tmp1_b, x2tb), writes=(ob,))
                dma("gp", d_out[512 * s + 128 * tm:512 * s + 128 * tm + 128, :],
                    ot[:, :], (ob,), (), "dg")
            if s < 3:
                tT_split(s + 1)
        fstack.close()

    stack0.close()
    S.final_wait("gp", "dg")

    sems = {}
    for e in ENGINES:
        sems[e] = nc.alloc_semaphore("sem_" + e)
    for k in S.dcnt:
        sems[k] = nc.alloc_semaphore("sem_" + k[2:])
    S.prepare()
    with nc.Block() as block:
        @block.tensor
        def _(t):
            S.emit_engine("pe", t, sems)

        @block.scalar
        def _(sc):
            S.emit_engine("act", sc, sems)

        @block.vector
        def _(v):
            S.emit_engine("dve", v, sems)

        @block.gpsimd
        def _(gp_):
            S.emit_engine("gp", gp_, sems)

        @block.sync
        def _(sy):
            S.emit_engine("sp", sy, sems)
    return nc


def _host_inputs(inputs, core):
    b, g = core // 2, core % 2
    x_b = np.ascontiguousarray(np.asarray(inputs["L0_attn_x"])[b])
    x_q = np.ascontiguousarray(x_b.reshape(32, 128, D)[g::2].reshape(OWN, D))
    maskA, maskB = _causal_masks(g)
    sel8 = np.zeros((8, 512), dtype=np.float32)
    for p in range(4):
        sel8[2 * p, 128 * p:128 * p + 64] = 1.0 / 1024.0
        sel8[2 * p + 1, 128 * p + 64:128 * p + 128] = 1.0 / 1024.0
    onesel = np.zeros((128, 64), dtype=np.float32)
    for q in range(8):
        onesel[:, 8 * q + q] = 1.0
    sel16 = np.zeros((16, 2048), dtype=np.float32)
    for e in range(16):
        sel16[e, 128 * e:128 * e + 128] = 1.0
    b1 = np.asarray(inputs["L0_ffn_b1"]).reshape(NEXP, 4, 128).transpose(0, 2, 1)
    bm = np.asarray(inputs["L0_ffn_bm"]).reshape(8, 128).T
    return {
        "x_q": x_q,
        "x_kv": x_b,
        "w_q": np.asarray(inputs["L0_attn_W_query"]),
        "w_k": np.asarray(inputs["L0_attn_W_key"]),
        "w_v": np.asarray(inputs["L0_attn_W_value"]),
        "w_p": np.asarray(inputs["L0_attn_W_proj"]),
        "w_m": np.asarray(inputs["L0_ffn_Wm"]),
        "w_o": np.asarray(inputs["L0_ffn_Wo"]),
        "w_1": np.asarray(inputs["L0_ffn_W1"]),
        "w_2": np.ascontiguousarray(np.asarray(inputs["L0_ffn_W2"]).reshape(NEXP, 4, 128, 128).transpose(0, 2, 1, 3).reshape(NEXP, 128, FI)),
        "b_1": np.ascontiguousarray(b1),
        "b_2": np.asarray(inputs["L0_ffn_b2"]),
        "b_m": np.ascontiguousarray(bm),
        "b_o": np.asarray(inputs["L0_ffn_bo"]).reshape(1, D),
        "emb": np.asarray(inputs["L0_ffn_expert_emb"]),
        "maskA": maskA,
        "maskB": maskB,
        "ident": np.eye(128, dtype=np.float32),
        "sel8": sel8,
        "onesel": onesel,
        "sel16": sel16,
    }


def kernel(**inputs):
    nc = build_nc("full")
    in_maps = [_host_inputs(inputs, core) for core in range(8)]
    res = run_bass_kernel_spmd(nc, in_maps, core_ids=list(range(8)))
    y = np.empty((B, T, D), dtype=np.float32)
    for core in range(8):
        b, g = core // 2, core % 2
        out = np.asarray(res.results[core]["out"])
        y[b].reshape(32, 128, D)[g::2] = out.reshape(NT, 128, D)
    return y
